# version 09
# baseline (speedup 1.0000x reference)
import os
import numpy as np
import concourse.bass as bass
import concourse.mybir as mybir
from concourse.bass_utils import run_bass_kernel_spmd
from contextlib import ExitStack

F32 = mybir.dt.float32
BF16 = mybir.dt.bfloat16
I32 = mybir.dt.int32
AF = mybir.ActivationFunctionType
ALU = mybir.AluOpType
AX = mybir.AxisListType

H = 32
D = 4096
N = 8192
R = 1024
DFF = 11008
FT = 86
NMC = 16
NG = 8
SCL = 0.08838834764831845
RMS_EPS = 1e-6
LN_EPS = 1e-5


class Prog:
    def __init__(self, sems):
        self.eng = {e: [] for e in ("tensor", "scalar", "vector", "sync", "gpsimd")}
        self.sems = sems
        self.cnt = {k: 0 for k in sems}

    def wait(self, e, sem, val):
        if val and val > 0:
            s = self.sems[sem]
            self.eng[e].append(lambda E, s=s, v=val: E.wait_ge(s, v))

    def op(self, e, fn, inc=None, inc_by=1):
        if inc is None:
            self.eng[e].append(fn)
            return None
        self.cnt[inc] += inc_by
        s = self.sems[inc]
        self.eng[e].append(lambda E, f=fn, s=s, v=inc_by: f(E).then_inc(s, v))
        return self.cnt[inc]

    def dma(self, e, out_ap, in_ap, inc):
        return self.op(e, lambda E, o=out_ap, i=in_ap: E.dma_start(out=o, in_=i),
                       inc=inc, inc_by=16)


def _flat(t):
    ap = t if isinstance(t, bass.AP) else t.ap()
    if len(ap.shape) == 3:
        return ap.rearrange("p a b -> p (a b)")
    return ap


def build_nc():
    nc = bass.Bass("TRN2", target_bir_lowering=False)

    xt_f = nc.dram_tensor("xt", [NMC, 128, 32, 512], F32, kind="ExternalInput")
    xr = nc.dram_tensor("xr", [R, D], F32, kind="ExternalInput")
    wq_t = nc.dram_tensor("wq_t", [NG, 128, 32, 512], F32, kind="ExternalInput")
    wk_t = nc.dram_tensor("wk_t", [NG, 128, 32, 512], F32, kind="ExternalInput")
    wv_t = nc.dram_tensor("wv_t", [NG, 128, 32, 512], F32, kind="ExternalInput")
    wo = nc.dram_tensor("wo", [D, D], F32, kind="ExternalInput")
    bq_l = nc.dram_tensor("bq_l", [128, 32], F32, kind="ExternalInput")
    bk_l = nc.dram_tensor("bk_l", [128, 32], F32, kind="ExternalInput")
    bv_b = nc.dram_tensor("bv_b", [128, D], F32, kind="ExternalInput")
    bo_r = nc.dram_tensor("bo_r", [1, D], F32, kind="ExternalInput")
    gq_c = nc.dram_tensor("gq_c", [128, 1], F32, kind="ExternalInput")
    gk_c = nc.dram_tensor("gk_c", [128, 1], F32, kind="ExternalInput")
    gw_t = nc.dram_tensor("gw_t", [FT, 128, 32, 128], I32, kind="ExternalInput")
    uw_t = nc.dram_tensor("uw_t", [FT, 128, 32, 128], I32, kind="ExternalInput")
    dw_t = nc.dram_tensor("dw_t", [FT, 128, D], I32, kind="ExternalInput")
    gs_l = nc.dram_tensor("gs_l", [128, FT], F32, kind="ExternalInput")
    us_l = nc.dram_tensor("us_l", [128, FT], F32, kind="ExternalInput")
    ds_r = nc.dram_tensor("ds_r", [1, D], F32, kind="ExternalInput")
    ones_bf = nc.dram_tensor("ones_bf", [128, 128], BF16, kind="ExternalInput")
    ones_f = nc.dram_tensor("ones_f", [1, 128], F32, kind="ExternalInput")
    ident = nc.dram_tensor("ident", [128, 128], BF16, kind="ExternalInput")
    eps_c = nc.dram_tensor("eps_c", [128, 2], F32, kind="ExternalInput")

    y = nc.dram_tensor("y", [R, D], F32, kind="ExternalOutput")

    _dbg = "ExternalOutput" if os.environ.get("KDBG") else "Internal"
    xt_bf = nc.dram_tensor("xt_bf", [NMC, 128, 32, 512], BF16, kind=_dbg)
    kt_d = nc.dram_tensor("kt_d", [D, N], BF16, kind=_dbg)
    v2_d = nc.dram_tensor("v2_d", [H, 128, 64, 128], BF16, kind=_dbg)
    qt_d = nc.dram_tensor("qt_d", [D, R], BF16, kind=_dbg)
    at_d = nc.dram_tensor("at_d", [D, R], BF16, kind=_dbg)
    h1_d = nc.dram_tensor("h1_d", [R, D], F32, kind=_dbg)
    q2_d = nc.dram_tensor("q2_d", [H, 1024], F32, kind=_dbg)
    op_d = nc.dram_tensor("op_d", [R, D], F32, kind=_dbg)
    st_d = nc.dram_tensor("st_d", [8, 8, 128], F32, kind=_dbg)
    pre_d = nc.dram_tensor("pre_d", [R, D], F32, kind=_dbg)
    wo_bf = nc.dram_tensor("wo_bf", [D, D], BF16, kind=_dbg)
    gw_bf = nc.dram_tensor("gw_bf", [FT, 128, 32, 128], BF16)
    uw_bf = nc.dram_tensor("uw_bf", [FT, 128, 32, 128], BF16)
    dw_bf = nc.dram_tensor("dw_bf", [FT, 128, D], BF16)

    root = ExitStack()
    with root:
        sems = {k: root.enter_context(nc.semaphore(k))
                for k in ("LD", "ST", "SG", "TE", "SC", "VE")}
        p = Prog(sems)

        def sb(stk, n, s, d):
            return stk.enter_context(nc.sbuf_tensor(n, s, d))

        def ps(stk, n, s, d):
            return stk.enter_context(nc.psum_tensor(n, s, d))

        s_ones_bf = sb(root, "c_ones_bf", [128, 128], BF16)
        s_ones_f = sb(root, "c_ones_f", [1, 128], F32)
        s_ident = sb(root, "c_ident", [128, 128], BF16)
        s_bq = sb(root, "c_bq", [128, 32], F32)
        s_bk = sb(root, "c_bk", [128, 32], F32)
        s_gq = sb(root, "c_gq", [128, 1], F32)
        s_gk = sb(root, "c_gk", [128, 1], F32)
        s_gs = sb(root, "c_gs", [128, FT], F32)
        s_us = sb(root, "c_us", [128, FT], F32)
        s_k2m = sb(root, "c_k2m", [1, H], F32)
        s_eps = sb(root, "c_eps", [128, 2], F32)

        st_consts = 0
        for src, dst in ((ones_bf, s_ones_bf), (ones_f, s_ones_f), (ident, s_ident),
                         (bq_l, s_bq), (bk_l, s_bk), (gq_c, s_gq),
                         (gk_c, s_gk), (gs_l, s_gs), (us_l, s_us),
                         (eps_c, s_eps)):
            st_consts = p.dma("sync", dst[:, :], src[:, :], inc="ST")

        cv_stack = ExitStack()
        s_cv = [sb(cv_stack, f"c_cv{i}", [128, 2048], BF16) for i in range(2)]
        conv_jobs = []
        for mc in range(NMC):
            sf = _flat(xt_f[mc, :, :, :])
            df = _flat(xt_bf[mc, :, :, :])
            for q in range(8):
                conv_jobs.append((sf[:, q * 2048:(q + 1) * 2048],
                                  df[:, q * 2048:(q + 1) * 2048],
                                  ("xt", mc)))
        for hdc in range(32):
            for q in range(2):
                conv_jobs.append((wo[128 * hdc:128 * (hdc + 1),
                                     q * 2048:(q + 1) * 2048],
                                  wo_bf[128 * hdc:128 * (hdc + 1),
                                        q * 2048:(q + 1) * 2048],
                                  "wo" if hdc == 31 and q == 1 else None))
        for ft in range(FT):
            for t_, tb in ((gw_t, gw_bf), (uw_t, uw_bf), (dw_t, dw_bf)):
                sf = _flat(t_[ft]) if len(t_.shape) == 4 else t_[ft, :, :]
                df = _flat(tb[ft]) if len(tb.shape) == 4 else tb[ft, :, :]
                for q in range(2):
                    conv_jobs.append((sf[:, q * 2048:(q + 1) * 2048],
                                      df[:, q * 2048:(q + 1) * 2048], None))
        cst = {"i": 0, "st": [0, 0], "wo_st": 0}
        xt_conv_st = [0] * NMC

        def emit_conv(k):
            for _ in range(k):
                i = cst["i"]
                if i >= len(conv_jobs):
                    return
                src, dst, tag = conv_jobs[i]
                b = i % 2
                w = src.shape[-1]
                p.wait("gpsimd", "SG", cst["st"][b])
                ld = p.dma("gpsimd", s_cv[b][:, 0:w], src, inc="LD")
                p.wait("gpsimd", "LD", ld)
                st = p.dma("gpsimd", dst, s_cv[b][:, 0:w], inc="SG")
                cst["st"][b] = st
                if tag == "wo":
                    cst["wo_st"] = st
                elif tag is not None:
                    xt_conv_st[tag[1]] = st
                cst["i"] = i + 1

        emit_conv(16)

        s1 = ExitStack()
        with s1:
            s_w = {nm: sb(s1, f"a_{nm}", [128, 32 * 512], BF16)
                   for nm in ("wq", "wk", "wv")}
            s_x = [sb(s1, f"a_x{i}", [128, 16 * 512], BF16) for i in range(4)]
            s_kraw = [sb(s1, f"a_kraw{i}", [128, 512], F32) for i in range(4)]
            s_ksq = [sb(s1, f"a_ksq{i}", [128, 512], BF16) for i in range(2)]
            s_t = [sb(s1, f"a_t{i}", [128, 512], F32) for i in range(1)]
            s_rr = [sb(s1, f"a_r{i}", [128, 512], F32) for i in range(2)]
            s_kn = [sb(s1, f"a_kn{i}", [128, 512], BF16) for i in range(4)]
            s_k2sq = [sb(s1, f"a_k2sq{i}", [128, 512], BF16) for i in range(2)]
            s_vn = [sb(s1, f"a_vn{i}", [128, 512], BF16) for i in range(4)]
            s_bvb = [sb(s1, f"a_bvb{i}", [128, 512], F32) for i in range(2)]
            s_red = sb(s1, "a_red", [1, 1], F32)
            s_q2r = [sb(s1, f"a_q2r{i}", [1, 512], F32) for i in range(2)]

            p_kv = [ps(s1, f"a_pkv{i}", [128, 512], F32) for i in range(4)]
            p_ms = [ps(s1, f"a_pms{i}", [128, 512], F32) for i in range(2)]
            p_k2 = ps(s1, "a_pk2", [1, 512], F32)

            trk = {
                "w_free_te": 0, "x_free_te": [0, 0, 0, 0], "wld": {},
                "bvb_ve": [0, 0], "bvb_st": [0, 0],
                "kraw_ve": [0, 0, 0, 0], "ksq_te": [0, 0], "ms_sc": [0, 0],
                "r_ve": [0, 0], "kn_st": [0, 0, 0, 0], "k2sq_te": [0, 0],
                "k2_ve": 0, "vn_st": [0, 0, 0, 0],
                "kv_free": [("SC", 0)] * 4, "kv_done": [0, 0, 0, 0],
                "par": 0, "q2r_st": [0, 0], "x_ld_st": {},
            }

            def rhs_x(it, dc):
                buf = (2 * it + dc // 16) % 4
                o = (dc % 16) * 512
                return s_x[buf][:, o:o + 512]

            def wave(wname, it):
                for h in range(4):
                    sem, cv = trk["kv_free"][h]
                    p.wait("tensor", sem, cv)
                    for dc in range(32):
                        lhs = s_w[wname][:, dc * 512 + h * 128:dc * 512 + (h + 1) * 128]
                        rhs = rhs_x(it, dc)
                        if dc < 31:
                            p.op("tensor", lambda E, o=p_kv[h][:, :], l=lhs, r=rhs,
                                 st=(dc == 0): E.matmul(o, l, r, start=st, stop=False))
                        else:
                            trk["kv_done"][h] = p.op(
                                "tensor", lambda E, o=p_kv[h][:, :], l=lhs, r=rhs:
                                E.matmul(o, l, r, start=False, stop=True), inc="TE")

            def evac_kraw(g, h, kind):
                head = 4 * g + h
                bias = (s_bk if kind == "k" else s_bq)[:, head:head + 1]
                p.wait("scalar", "TE", trk["kv_done"][h])
                p.wait("scalar", "VE", trk["kraw_ve"][h])
                sc_kraw = p.op("scalar", lambda E, o=s_kraw[h][:, :],
                               i=p_kv[h][:, :], bb=bias:
                               E.activation(o, i, AF.Identity, bias=bb), inc="SC")
                trk["kv_free"][h] = ("SC", sc_kraw)
                return sc_kraw

            def norm_rest(g, mc, h, kind, sc_kraw):
                head = 4 * g + h
                b2 = trk["par"] % 2
                trk["par"] += 1
                gvec = s_gk if kind == "k" else s_gq
                p.wait("vector", "SC", sc_kraw)
                p.wait("vector", "TE", trk["ksq_te"][b2])
                ve_ksq = p.op("vector", lambda E, o=s_ksq[b2][:, :],
                              a=s_kraw[h][:, :]: E.tensor_mul(o, a, a), inc="VE")
                p.wait("tensor", "VE", ve_ksq)
                p.wait("tensor", "SC", trk["ms_sc"][b2])
                te_ms = p.op("tensor", lambda E, o=p_ms[b2][:, :],
                             l=s_ones_bf[:, :], r=s_ksq[b2][:, :]:
                             E.matmul(o, l, r, start=True, stop=True), inc="TE")
                trk["ksq_te"][b2] = te_ms
                p.wait("scalar", "TE", te_ms)
                p.wait("scalar", "VE", trk["r_ve"][b2])
                p.op("scalar", lambda E, o=s_t[0][:, :], i=p_ms[b2][:, :]:
                     E.activation(o, i, AF.Ln, bias=s_eps[:, 0:1],
                                  scale=1.0 / 128.0))
                sc_r = p.op("scalar", lambda E, o=s_rr[b2][:, :], i=s_t[0][:, :]:
                            E.activation(o, i, AF.Exp, scale=-0.5), inc="SC")
                trk["ms_sc"][b2] = sc_r
                p.wait("vector", "SC", sc_r)
                p.wait("vector", "ST", trk["kn_st"][h])
                ve_kn = p.op("vector", lambda E, o=s_kn[h][:, :],
                             a=s_kraw[h][:, :], gg=gvec[:, :], rr2=s_rr[b2][:, :]:
                             E.scalar_tensor_tensor(o, a, gg, rr2, ALU.mult,
                                                    ALU.mult), inc="VE")
                trk["kraw_ve"][h] = ve_kn
                trk["r_ve"][b2] = ve_kn
                p.wait("vector", "TE", trk["k2sq_te"][b2])
                ve_k2 = p.op("vector", lambda E, o=s_k2sq[b2][:, :],
                             a=s_kn[h][:, :]: E.tensor_mul(o, a, a), inc="VE")
                p.wait("tensor", "VE", ve_k2)
                p.wait("tensor", "VE", trk["k2_ve"])
                te_k2 = p.op("tensor", lambda E, o=p_k2[:, :],
                             l=s_ones_bf[:, 0:1], r=s_k2sq[b2][:, :]:
                             E.matmul(o, l, r, start=True, stop=True), inc="TE")
                trk["k2sq_te"][b2] = te_k2
                p.wait("vector", "TE", te_k2)
                if kind == "k":
                    ve_red = p.op("vector", lambda E, o=s_red[:, :], i=p_k2[:, :]:
                                  E.reduce_max(o, i, axis=AX.X), inc="VE")
                    p.wait("vector", "VE", ve_red)
                    if mc == 0:
                        ve_m = p.op("vector", lambda E,
                                    o=s_k2m[0:1, head:head + 1], i=s_red[:, :]:
                                    E.tensor_copy(o, i), inc="VE")
                    else:
                        ve_m = p.op("vector", lambda E,
                                    o=s_k2m[0:1, head:head + 1], i=s_red[:, :]:
                                    E.tensor_max(o, o, i), inc="VE")
                else:
                    qb = trk["par"] % 2
                    p.wait("vector", "ST", trk["q2r_st"][qb])
                    ve_m = p.op("vector", lambda E, o=s_q2r[qb][:, :],
                                i=p_k2[:, :]: E.tensor_copy(o, i), inc="VE")
                    p.wait("sync", "VE", ve_m)
                    trk["q2r_st"][qb] = p.dma(
                        "sync", q2_d[head:head + 1, mc * 512:mc * 512 + 512],
                        s_q2r[qb][:, :], inc="ST")
                trk["k2_ve"] = ve_m
                dst = kt_d if kind == "k" else qt_d
                col = mc * 512
                p.wait("sync", "VE", ve_kn)
                trk["kn_st"][h] = p.dma(
                    "sync", dst[128 * head:128 * (head + 1), col:col + 512],
                    s_kn[h][:, :], inc="ST")

            def load_x_halves(it):
                if it >= NG * NMC:
                    return
                mc = it % NMC
                assert xt_conv_st[mc] > 0, "conv emission fell behind"
                flat = _flat(xt_bf[mc, :, :, :])
                for half in range(2):
                    buf = (2 * it + half) % 4
                    p.wait("sync", "SG", xt_conv_st[mc])
                    p.wait("sync", "TE", trk["x_free_te"][buf])
                    trk["x_ld_st"][it] = p.dma(
                        "sync", s_x[buf][:, :],
                        flat[:, half * 8192:(half + 1) * 8192], inc="ST")

            load_x_halves(0)
            for g in range(NG):
                p.wait("gpsimd", "TE", trk["w_free_te"])
                for nm, t in (("wk", wk_t), ("wv", wv_t), ("wq", wq_t)):
                    trk["wld"][nm] = p.dma("gpsimd", s_w[nm][:, :],
                                           _flat(t[g, :, :, :]), inc="LD")
                gb = g % 2
                p.wait("sync", "VE", trk["bvb_ve"][gb])
                trk["bvb_st"][gb] = p.dma(
                    "sync", s_bvb[gb][:, :], bv_b[:, 512 * g:512 * (g + 1)],
                    inc="ST")
                for mc in range(NMC):
                    it = g * NMC + mc
                    load_x_halves(it + 1)
                    p.wait("tensor", "ST", trk["x_ld_st"][it])
                    if mc == 0:
                        p.wait("tensor", "LD", trk["wld"]["wk"])
                    wave("wk", it)
                    sc_k = [evac_kraw(g, h, "k") for h in range(4)]
                    if mc == 0:
                        p.wait("tensor", "LD", trk["wld"]["wv"])
                        p.wait("tensor", "ST", trk["bvb_st"][g % 2])
                    for mt in range(4):
                        sem, cv = trk["kv_free"][mt]
                        p.wait("tensor", sem, cv)
                        for dc in range(32):
                            lhs_x = rhs_x(it, dc)[:, mt * 128:(mt + 1) * 128]
                            rhs_w = s_w["wv"][:, dc * 512:(dc + 1) * 512]
                            if dc < 31:
                                p.op("tensor", lambda E, o=p_kv[mt][:, :],
                                     l=lhs_x, r=rhs_w, st=(dc == 0):
                                     E.matmul(o, l, r, start=st, stop=False))
                            else:
                                trk["kv_done"][mt] = p.op(
                                    "tensor", lambda E, o=p_kv[mt][:, :],
                                    l=lhs_x, r=rhs_w:
                                    E.matmul(o, l, r, start=False, stop=True),
                                    inc="TE")
                    for mt in range(4):
                        mtg = mc * 4 + mt
                        p.wait("vector", "TE", trk["kv_done"][mt])
                        p.wait("vector", "ST", trk["vn_st"][mt])
                        ve_v = p.op("vector", lambda E, o=s_vn[mt][:, :],
                                    i=p_kv[mt][:, :], bb=s_bvb[g % 2][:, :]:
                                    E.tensor_add(o, i, bb), inc="VE")
                        trk["kv_free"][mt] = ("VE", ve_v)
                        p.wait("sync", "VE", ve_v)
                        st_v = 0
                        for h in range(4):
                            head = 4 * g + h
                            st_v = p.dma(
                                "sync",
                                v2_d[head, :, mtg, :].rearrange("p hd -> p 1 hd")
                                if False else v2_d[head, :, mtg:mtg + 1, :],
                                s_vn[mt].ap().rearrange("p (h hd) -> p h hd",
                                                        hd=128)[:, h:h + 1, :],
                                inc="ST")
                        trk["vn_st"][mt] = st_v
                    trk["bvb_ve"][g % 2] = max(trk["bvb_ve"][g % 2],
                                               p.cnt["VE"])
                    for h in range(4):
                        norm_rest(g, mc, h, "k", sc_k[h])
                    if mc < 2:
                        if mc == 0:
                            p.wait("tensor", "LD", trk["wld"]["wq"])
                        wave("wq", it)
                        sc_q = [evac_kraw(g, h, "q") for h in range(4)]
                        for h in range(4):
                            norm_rest(g, mc, h, "q", sc_q[h])
                    trk["x_free_te"][(2 * it) % 4] = p.cnt["TE"]
                    trk["x_free_te"][(2 * it + 1) % 4] = p.cnt["TE"]
                    emit_conv(8 if it < 16 else 2)
                trk["w_free_te"] = p.cnt["TE"]

            s1_te, s1_sc, s1_ve, s1_st = (p.cnt["TE"], p.cnt["SC"],
                                          p.cnt["VE"], p.cnt["ST"])


        s2 = ExitStack()
        with s2:
            s_kt = [sb(s2, f"b_kt{i}", [128, N], BF16) for i in range(2)]
            s_vna = [sb(s2, f"b_vnat{i}", [128, N], BF16) for i in range(2)]
            s_qt = [sb(s2, f"b_qt{i}", [128, R], BF16) for i in range(2)]
            s_E = [sb(s2, f"b_E{i}", [128, 1024], BF16) for i in range(2)]
            s_rS = sb(s2, "b_rS", [128, 512], F32)
            s_oT = sb(s2, "b_outT", [128, R], BF16)
            s_Bt = [sb(s2, f"b_Bt{i}", [1, R], F32) for i in range(2)]
            s_nb = sb(s2, "b_negb", [1, 2], F32)
            s_nB = sb(s2, "b_negB", [128, 2], F32)

            p_sc = [ps(s2, f"b_psc{i}", [128, 1024], F32) for i in range(2)]
            p_pv = [ps(s2, f"b_ppv{i}", [128, 512], F32) for i in range(2)]
            p_S = [ps(s2, f"b_pS{i}", [128, 512], F32) for i in range(2)]

            t2 = {"kvq_te": [s1_te, s1_te],
                  "E_te": [0, 0], "sc_sc": [s1_sc, s1_sc], "oT_st": 0,
                  "pvS_ve": s1_ve}

            p.wait("tensor", "SC", s1_sc)
            p.wait("tensor", "VE", s1_ve)

            def head_loads(h):
                hb = h % 2
                p.wait("sync", "ST", s1_st)
                p.wait("sync", "TE", t2["kvq_te"][hb])
                p.dma("sync", s_kt[hb][:, :], kt_d[128 * h:128 * (h + 1), :],
                      inc="ST")
                p.dma("sync", s_vna[hb][:, :], _flat(v2_d[h, :, :, :]), inc="ST")
                p.dma("sync", s_qt[hb][:, :], qt_d[128 * h:128 * (h + 1), :],
                      inc="ST")
                p.wait("sync", "VE", t2["bt_ve"][hb])
                t2["ld"][hb] = p.dma("sync", s_Bt[hb][:, :], q2_d[h:h + 1, :],
                                     inc="ST")

            t2["ld"] = [0, 0]
            t2["bt_ve"] = [0, 0]
            head_loads(0)
            for h in range(H):
                hb = h % 2
                if h + 1 < H:
                    head_loads(h + 1)
                ld_k = ld_v = ld_q = t2["ld"][hb]

                if h == 0:
                    p.wait("vector", "SC", s1_sc)
                p.wait("vector", "ST", ld_q)
                ve_bt = p.op("vector", lambda E, o=s_Bt[hb][:, :],
                             i=s_Bt[hb][:, :], km=s_k2m[0:1, h:h + 1]:
                             E.tensor_scalar(o, i, km, 1.0 / 128.0, ALU.mult,
                                             ALU.mult), inc="VE")
                p.wait("scalar", "VE", ve_bt)
                p.op("scalar", lambda E, o=s_Bt[hb][:, :], i=s_Bt[hb][:, :]:
                     E.activation(o, i, AF.Ln))
                sc_bt = p.op("scalar", lambda E, o=s_Bt[hb][:, :],
                             i=s_Bt[hb][:, :]:
                             E.activation(o, i, AF.Exp, scale=0.5), inc="SC")
                p.wait("vector", "SC", sc_bt)
                ve_nb0 = p.op("vector", lambda E, o=s_nb[0:1, 0:1],
                              i=s_Bt[hb][0:1, 0:512]:
                              E.reduce_max(o, i, axis=AX.X, negate=True), inc="VE")
                p.wait("vector", "VE", ve_nb0)
                ve_nb = p.op("vector", lambda E, o=s_nb[0:1, 1:2],
                             i=s_Bt[hb][0:1, 512:1024]:
                             E.reduce_max(o, i, axis=AX.X, negate=True), inc="VE")
                t2["bt_ve"][hb] = ve_nb
                p.wait("tensor", "VE", ve_nb)
                p.wait("tensor", "VE", t2["pvS_ve"])
                te_nb = p.op("tensor", lambda E, o=p_pv[0][:, 0:2],
                             l=s_ones_f[:, :], r=s_nb[:, :]:
                             E.matmul(o, l, r, start=True, stop=True), inc="TE")
                p.wait("vector", "TE", te_nb)
                ve_nB = p.op("vector", lambda E, o=s_nB[:, :], i=p_pv[0][:, 0:2]:
                             E.tensor_copy(o, i), inc="VE")

                p.wait("tensor", "ST", ld_q)
                ve_vnat = ve_nB
                te_attn = [0, 0]
                for bq in range(32):
                    for ncx in range(2):
                        eb = ncx
                        p.wait("tensor", "SC", t2["sc_sc"][eb])
                        te_sc_last = 0
                        for k in range(2):
                            mt = 2 * bq + k
                            lhs = s_kt[hb][:, mt * 128:(mt + 1) * 128]
                            rhs = s_qt[hb][:, ncx * 512:(ncx + 1) * 512]
                            oo = p_sc[eb][:, k * 512:(k + 1) * 512]
                            fn = (lambda E, o=oo, l=lhs, r=rhs:
                                  E.matmul(o, l, r, start=True, stop=True))
                            te_sc_last = p.op("tensor", fn,
                                              inc="TE" if k == 1 else None) \
                                or te_sc_last
                        p.wait("scalar", "TE", te_sc_last)
                        p.wait("scalar", "TE", t2["E_te"][eb])
                        if bq == 0 and ncx == 0:
                            p.wait("scalar", "VE", ve_nB)
                        sc_e = p.op("scalar", lambda E, o=s_E[eb][:, :],
                                    i=p_sc[eb][:, :], bb=s_nB[:, ncx:ncx + 1]:
                                    E.activation(o, i, AF.Exp, bias=bb, scale=SCL),
                                    inc="SC")
                        t2["sc_sc"][eb] = sc_e
                        p.wait("tensor", "SC", sc_e)
                        if bq == 0:
                            p.wait("tensor", "VE", ve_vnat)
                            p.wait("tensor", "VE", t2["pvS_ve"])
                        first = (bq == 0)
                        last = (bq == 31)
                        for k in range(2):
                            mt = 2 * bq + k
                            p.op("tensor", lambda E, o=p_pv[ncx][:, :],
                                 l=s_vna[hb][:, mt * 128:(mt + 1) * 128],
                                 r=s_E[eb][:, k * 512:(k + 1) * 512],
                                 st=(first and k == 0), sp=(last and k == 1):
                                 E.matmul(o, l, r, start=st, stop=sp))
                        for k in range(2):
                            fn = (lambda E, o=p_S[ncx][:, :], l=s_ones_bf[:, :],
                                  r=s_E[eb][:, k * 512:(k + 1) * 512],
                                  st=(first and k == 0), sp=(last and k == 1):
                                  E.matmul(o, l, r, start=st, stop=sp))
                            if last and k == 1:
                                te_attn[ncx] = p.op("tensor", fn, inc="TE")
                            else:
                                p.op("tensor", fn)
                        t2["E_te"][eb] = p.cnt["TE"]
                for ncx in range(2):
                    p.wait("vector", "TE", te_attn[ncx])
                    p.op("vector", lambda E, o=s_rS[:, :], i=p_S[ncx][:, :]:
                         E.reciprocal(o, i))
                    p.wait("vector", "ST", t2["oT_st"])
                    t2["pvS_ve"] = p.op(
                        "vector", lambda E, o=s_oT[:, ncx * 512:(ncx + 1) * 512],
                        a=p_pv[ncx][:, :], b=s_rS[:, :]: E.tensor_mul(o, a, b),
                        inc="VE")
                t2["kvq_te"][hb] = p.cnt["TE"]
                p.wait("sync", "VE", t2["pvS_ve"])
                t2["oT_st"] = p.dma("sync", at_d[128 * h:128 * (h + 1), :],
                                    s_oT[:, :], inc="ST")
                emit_conv(12)

            s2_te, s2_sc, s2_ve, s2_st = (p.cnt["TE"], p.cnt["SC"],
                                          p.cnt["VE"], p.cnt["ST"])
        while cst["i"] < len(conv_jobs):
            emit_conv(16)
        conv_st = p.cnt["SG"]
        wo_conv_st = cst["wo_st"]
        cv_stack.close()

        s3 = ExitStack()
        with s3:
            s_acc = sb(s3, "d_acc", [128, 4 * D], F32)
            s_h1T = sb(s3, "d_h1T", [128, 32 * 512], BF16)
            s_aux = sb(s3, "d_aux", [128, D], F32)
            s_bo_b = sb(s3, "d_bo_b", [128, D], BF16)
            s_ds_b = sb(s3, "d_ds_b", [128, D], BF16)
            lnb = {nm: sb(s3, f"d_ln_{nm}", [128, 1], F32)
                   for nm in ("sum", "ssq", "m", "t3", "var", "rstd", "lb")}

            p_op = [ps(s3, f"d_pop{i}", [128, 1024], F32) for i in range(2)]
            p_g = ps(s3, "d_pg", [128, 512], F32)
            p_u = ps(s3, "d_pu", [128, 512], F32)
            p_tp3 = ps(s3, "d_ptp3", [128, 128], BF16)

            t3 = {"op_ve": [s2_ve, s2_ve], "aux_ve": s2_ve,
                  "tp3_ve": s2_ve, "h1bf_te": s2_te, "h1bf_sc": s2_sc,
                  "h1_st": [0] * 8, "acc_free_st": 0, "bcast_ve": s2_ve}

            p.wait("tensor", "SC", s2_sc)
            p.wait("tensor", "VE", s2_ve)

            bc = ExitStack()
            with bc:
                s_row = sb(bc, "d_row", [1, D], F32)
                for which, (rowsrc, dstb) in enumerate(((bo_r, s_bo_b),
                                                        (ds_r, s_ds_b))):
                    p.wait("sync", "VE", t3["bcast_ve"] if which else 0)
                    row_ld = p.dma("sync", s_row[:, :], rowsrc[:, :], inc="ST")
                    for j4 in range(4):
                        cc = j4 * 1024
                        p.wait("tensor", "ST", row_ld)
                        p.wait("tensor", "VE", t3["bcast_ve"])
                        p.op("tensor", lambda E, o=p_op[1][:, 0:512],
                             l=s_ones_f[:, :], r=s_row[0:1, cc:cc + 512]:
                             E.matmul(o, l, r, start=True, stop=True))
                        te_b = p.op("tensor", lambda E, o=p_op[1][:, 512:1024],
                                    l=s_ones_f[:, :],
                                    r=s_row[0:1, cc + 512:cc + 1024]:
                                    E.matmul(o, l, r, start=True, stop=True),
                                    inc="TE")
                        p.wait("vector", "TE", te_b)
                        t3["bcast_ve"] = p.op(
                            "vector", lambda E, o=dstb[:, cc:cc + 1024],
                            i=p_op[1][:, :]: E.tensor_copy(o, i), inc="VE")
            ve_bcast = t3["bcast_ve"]
            t3["op_ve"] = [ve_bcast, ve_bcast]

            GDN = 3
            groups = [list(range(s, min(s + GDN, FT))) for s in range(0, FT, GDN)]

            def gate_counts():
                return dict(TE=p.cnt["TE"], SC=p.cnt["SC"], VE=p.cnt["VE"],
                            ST=p.cnt["ST"], LD=p.cnt["LD"])

            def gate_wait(e, gc):
                for s, v in gc.items():
                    p.wait(e, s, v)

            for half in range(2):
                rbase = half * 512
                gc = gate_counts()
                op_stack = ExitStack()
                with op_stack:
                    s_at = [sb(op_stack, f"d_at{half}{i}", [128, 4 * 512], BF16)
                            for i in range(2)]
                    s_wo4 = [sb(op_stack, f"d_wo{half}{i}", [128, 4 * 2048], BF16)
                             for i in range(2)]
                    s_h1bf = sb(op_stack, f"d_h1bf{half}", [128, D], BF16)
                    for e in ("tensor", "scalar", "vector", "sync"):
                        gate_wait(e, gc)
                    op_trk = {"at_te": [0, 0]}
                    p.wait("sync", "ST", s2_st)
                    p.wait("sync", "SG", wo_conv_st)
                    for dmh in range(2):
                        for gg in range(8):
                            wb = gg % 2
                            p.wait("sync", "TE", op_trk["at_te"][wb])
                            for j in range(4):
                                hdc = gg * 4 + j
                                p.dma("sync", s_at[wb][:, j * 512:(j + 1) * 512],
                                      at_d[128 * hdc:128 * (hdc + 1),
                                           rbase:rbase + 512], inc="ST")
                            wo_ld = 0
                            for j in range(4):
                                hdc = gg * 4 + j
                                wo_ld = p.dma(
                                    "sync", s_wo4[wb][:, j * 2048:(j + 1) * 2048],
                                    wo_bf[128 * hdc:128 * (hdc + 1),
                                          dmh * 2048:(dmh + 1) * 2048], inc="ST")
                            p.wait("tensor", "ST", wo_ld)
                            for rt in range(4):
                                for u in range(2):
                                    ub = u % 2
                                    p.wait("tensor", "VE", t3["op_ve"][ub])
                                    te_u = 0
                                    for j in range(4):
                                        for kk in range(2):
                                            lhs = s_at[wb][:, j * 512 + rt * 128:
                                                           j * 512 + (rt + 1) * 128]
                                            rhs = s_wo4[wb][
                                                :, j * 2048 + u * 1024 + kk * 512:
                                                j * 2048 + u * 1024 + (kk + 1) * 512]
                                            oo = p_op[ub][:, kk * 512:(kk + 1) * 512]
                                            fn = (lambda E, o=oo, l=lhs, r=rhs,
                                                  st=(j == 0), sp=(j == 3):
                                                  E.matmul(o, l, r, start=st,
                                                           stop=sp))
                                            if j == 3 and kk == 1:
                                                te_u = p.op("tensor", fn, inc="TE")
                                            else:
                                                p.op("tensor", fn)
                                    p.wait("vector", "TE", te_u)
                                    accsl = s_acc[:, rt * D + dmh * 2048 + u * 1024:
                                                  rt * D + dmh * 2048 +
                                                  (u + 1) * 1024]
                                    if gg == 0:
                                        p.wait("vector", "ST", t3["acc_free_st"])
                                        t3["op_ve"][ub] = p.op(
                                            "vector", lambda E, o=accsl,
                                            i=p_op[ub][:, :]: E.tensor_copy(o, i),
                                            inc="VE")
                                    else:
                                        t3["op_ve"][ub] = p.op(
                                            "vector", lambda E, o=accsl,
                                            i=p_op[ub][:, :]: E.tensor_add(o, o, i),
                                            inc="VE")
                            op_trk["at_te"][wb] = p.cnt["TE"]

                    def ln_block(rt, half, is_ln2, dump):
                        accsl = s_acc[:, rt * D:(rt + 1) * D]
                        row_g = half * 4 + rt
                        p.wait("sync", "VE", t3["aux_ve"])
                        if is_ln2:
                            p.wait("sync", "ST", t3["h1_st"][4 * half + rt])
                            src_d = h1_d
                        else:
                            src_d = xr
                        ld_a = p.dma("sync", s_aux[:, :],
                                     src_d[128 * row_g:128 * (row_g + 1), :],
                                     inc="ST")
                        if rt == 0:
                            p.wait("vector", "VE", t3["op_ve"][0])
                            p.wait("vector", "VE", t3["op_ve"][1])
                        if is_ln2:
                            p.op("vector", lambda E, o=accsl, b=s_ds_b[:, :]:
                                 E.tensor_mul(o, o, b))
                        else:
                            p.op("vector", lambda E, o=accsl, b=s_bo_b[:, :]:
                                 E.tensor_add(o, o, b))
                        p.wait("vector", "ST", ld_a)
                        t3["aux_ve"] = p.op("vector", lambda E, o=accsl,
                                            b=s_aux[:, :]: E.tensor_add(o, o, b),
                                            inc="VE")
                        p.wait("scalar", "VE", t3["aux_ve"])
                        p.wait("scalar", "TE", t3["h1bf_te"])
                        sc_sum = p.op("scalar", lambda E, o=dump, i=accsl,
                                      ao=lnb["sum"][:, :]:
                                      E.activation(o, i, AF.Identity, accum_out=ao),
                                      inc="SC")
                        t3["h1bf_sc"] = sc_sum
                        sc_ssq = p.op("scalar", lambda E, o=dump, i=accsl,
                                      ao=lnb["ssq"][:, :]:
                                      E.activation(o, i, AF.Square, accum_out=ao),
                                      inc="SC")
                        p.wait("vector", "SC", sc_ssq)
                        ve_m = p.op("vector", lambda E, o=lnb["m"][:, :],
                                    i=lnb["sum"][:, :]:
                                    E.tensor_scalar_mul(o, i, 1.0 / D), inc="VE")
                        p.wait("vector", "VE", ve_m)
                        ve_t3 = p.op("vector", lambda E, o=lnb["t3"][:, :],
                                     i=lnb["m"][:, :]: E.tensor_mul(o, i, i),
                                     inc="VE")
                        p.wait("vector", "VE", ve_t3)
                        ve_var = p.op("vector", lambda E, o=lnb["var"][:, :],
                                      i=lnb["ssq"][:, :], t=lnb["t3"][:, :]:
                                      E.scalar_tensor_tensor(o, i, 1.0 / D, t,
                                                             ALU.mult,
                                                             ALU.subtract),
                                      inc="VE")
                        p.wait("scalar", "VE", ve_var)
                        sc_ln = p.op("scalar", lambda E, o=lnb["rstd"][:, :],
                                     i=lnb["var"][:, :]:
                                     E.activation(o, i, AF.Ln, bias=s_eps[:, 1:2]),
                                     inc="SC")
                        p.wait("scalar", "SC", sc_ln)
                        sc_rstd = p.op("scalar", lambda E, o=lnb["rstd"][:, :],
                                       i=lnb["rstd"][:, :]:
                                       E.activation(o, i, AF.Exp, scale=-0.5),
                                       inc="SC")
                        p.wait("vector", "SC", sc_rstd)
                        ve_lb = p.op("vector", lambda E, o=lnb["lb"][:, :],
                                     i=lnb["m"][:, :], rs=lnb["rstd"][:, :]:
                                     E.scalar_tensor_tensor(o, i, -1.0, rs,
                                                            ALU.mult, ALU.mult),
                                     inc="VE")
                        p.wait("scalar", "VE", ve_lb)
                        p.wait("scalar", "SC", sc_rstd)
                        if os.environ.get("KDBG") and not is_ln2:
                            p.wait("sync", "VE", ve_lb)
                            for ki, nm in enumerate(("sum", "ssq", "m", "t3",
                                                     "var", "rstd", "lb")):
                                p.dma("sync", st_d[row_g, ki, :].rearrange(
                                    "(p o) -> p o", o=1), lnb[nm][:, :], inc="ST")
                            dbg_st = p.dma("sync",
                                           pre_d[128 * row_g:128 * (row_g + 1), :],
                                           accsl, inc="ST")
                            p.wait("scalar", "ST", dbg_st)
                        sc_norm = p.op("scalar", lambda E, o=accsl, i=accsl,
                                       bb=lnb["lb"][:, :], ss=lnb["rstd"][:, :]:
                                       E.activation(o, i, AF.Identity, bias=bb,
                                                    scale=ss), inc="SC")
                        return accsl, row_g, sc_norm

                    if os.environ.get("KDBG"):
                        for rt in range(4):
                            row_g = half * 4 + rt
                            p.wait("sync", "VE", t3["op_ve"][0])
                            p.wait("sync", "VE", t3["op_ve"][1])
                            p.dma("sync", op_d[128 * row_g:128 * (row_g + 1), :],
                                  s_acc[:, rt * D:(rt + 1) * D], inc="ST")
                    for rt in range(4):
                        accsl, row_g, sc_norm = ln_block(rt, half, False,
                                                          s_h1bf[:, :])
                        p.wait("sync", "SC", sc_norm)
                        t3["h1_st"][4 * half + rt] = p.dma(
                            "sync", h1_d[128 * row_g:128 * (row_g + 1), :], accsl,
                            inc="ST")
                        p.wait("vector", "SC", sc_norm)
                        ve_h1bf = p.op("vector", lambda E, o=s_h1bf[:, :],
                                       i=accsl: E.tensor_copy(o, i), inc="VE")
                        p.wait("tensor", "VE", ve_h1bf)
                        for dc in range(32):
                            p.wait("tensor", "VE", t3["tp3_ve"])
                            te_tp = p.op("tensor", lambda E, o=p_tp3[:, :],
                                         i=s_h1bf[:, dc * 128:(dc + 1) * 128],
                                         idd=s_ident[:, :]:
                                         E.transpose(o, i, idd), inc="TE")
                            p.wait("vector", "TE", te_tp)
                            t3["tp3_ve"] = p.op(
                                "vector", lambda E,
                                o=s_h1T[:, dc * 512 + rt * 128:
                                        dc * 512 + (rt + 1) * 128],
                                i=p_tp3[:, :]: E.tensor_copy(o, i), inc="VE")
                        t3["h1bf_te"] = p.cnt["TE"]
                    ve_h1T_done = t3["tp3_ve"]

                gc = gate_counts()
                ffn_stack = ExitStack()
                with ffn_stack:
                    s_gw = [sb(ffn_stack, f"d_gw{half}{i}", [128, 32 * 128], BF16)
                            for i in range(2)]
                    s_uw = [sb(ffn_stack, f"d_uw{half}{i}", [128, 32 * 128], BF16)
                            for i in range(2)]
                    s_dwh = [sb(ffn_stack, f"d_dwh{half}{i}", [128, 2048], BF16)
                             for i in range(7)]
                    s_gsil = sb(ffn_stack, f"d_gsil{half}", [128, 512], F32)
                    s_sup = sb(ffn_stack, f"d_sup{half}", [128, 512], F32)
                    s_hT = [sb(ffn_stack, f"d_hT{half}{i}", [128, 512], BF16)
                            for i in range(5)]
                    for e in ("tensor", "scalar", "vector", "sync"):
                        gate_wait(e, gc)
                    f_trk = {"gw_te": [0, 0], "dw_te": [0] * 7, "hT_te": [0] * 5,
                             "gu_sc": [0, 0], "gs_ve": 0}
                    for gi, grp in enumerate(groups):
                        for ft in grp:
                            fb = ft % 2
                            if gi == 0 and ft == 0:
                                p.wait("sync", "SG", conv_st)
                            p.wait("sync", "TE", f_trk["gw_te"][fb])
                            p.dma("sync", s_gw[fb][:, :],
                                  _flat(gw_bf[ft, :, :, :]), inc="ST")
                            uw_ld = p.dma("sync", s_uw[fb][:, :],
                                          _flat(uw_bf[ft, :, :, :]), inc="ST")
                            for dh in range(2):
                                di = (2 * ft + dh) % 7
                                p.wait("sync", "TE", f_trk["dw_te"][di])
                                p.dma("sync", s_dwh[di][:, :],
                                      dw_bf[ft, :, dh * 2048:(dh + 1) * 2048],
                                      inc="ST")
                            p.wait("tensor", "ST", uw_ld)
                            if gi == 0 and ft == grp[0]:
                                p.wait("tensor", "VE", ve_h1T_done)
                            p.wait("tensor", "SC", f_trk["gu_sc"][0])
                            te_g = 0
                            for dc in range(32):
                                lhs = s_gw[fb][:, dc * 128:(dc + 1) * 128]
                                rhs = s_h1T[:, dc * 512:dc * 512 + 512]
                                fn = (lambda E, o=p_g[:, :], l=lhs, r=rhs,
                                      st=(dc == 0), sp=(dc == 31):
                                      E.matmul(o, l, r, start=st, stop=sp))
                                te_g = p.op("tensor", fn,
                                            inc="TE" if dc == 31 else None) or te_g
                            p.wait("tensor", "SC", f_trk["gu_sc"][1])
                            te_up = 0
                            for dc in range(32):
                                lhs = s_uw[fb][:, dc * 128:(dc + 1) * 128]
                                rhs = s_h1T[:, dc * 512:dc * 512 + 512]
                                fn = (lambda E, o=p_u[:, :], l=lhs, r=rhs,
                                      st=(dc == 0), sp=(dc == 31):
                                      E.matmul(o, l, r, start=st, stop=sp))
                                te_up = p.op("tensor", fn,
                                             inc="TE" if dc == 31 else None) \
                                    or te_up
                            f_trk["gw_te"][fb] = p.cnt["TE"]
                            p.wait("scalar", "TE", te_g)
                            p.wait("scalar", "VE", f_trk["gs_ve"])
                            sc_g = p.op("scalar", lambda E, o=s_gsil[:, :],
                                        i=p_g[:, :], ss=s_gs[:, ft:ft + 1]:
                                        E.activation(o, i, AF.Silu, scale=ss),
                                        inc="SC")
                            f_trk["gu_sc"][0] = sc_g
                            p.wait("scalar", "TE", te_up)
                            sc_u = p.op("scalar", lambda E, o=s_sup[:, :],
                                        i=p_u[:, :], ss=s_us[:, ft:ft + 1]:
                                        E.activation(o, i, AF.Identity, scale=ss),
                                        inc="SC")
                            f_trk["gu_sc"][1] = sc_u
                            p.wait("vector", "SC", sc_u)
                            p.wait("vector", "TE", f_trk["hT_te"][ft % 5])
                            f_trk["gs_ve"] = p.op(
                                "vector", lambda E, o=s_hT[ft % 5][:, :],
                                a=s_gsil[:, :], b=s_sup[:, :]:
                                E.tensor_mul(o, a, b), inc="VE")
                        ve_h_grp = f_trk["gs_ve"]
                        p.wait("tensor", "VE", ve_h_grp)
                        nj = len(grp)
                        for rt in range(4):
                            for u in range(4):
                                ub = u % 2
                                p.wait("tensor", "VE", t3["op_ve"][ub])
                                te_u2 = 0
                                for j, ft in enumerate(grp):
                                    di = (2 * ft + (u // 2)) % 7
                                    for kk in range(2):
                                        lhs = s_hT[ft % 5][:, rt * 128:
                                                           (rt + 1) * 128]
                                        rhs = s_dwh[di][:, (u % 2) * 1024 +
                                                        kk * 512:
                                                        (u % 2) * 1024 +
                                                        (kk + 1) * 512]
                                        oo = p_op[ub][:, kk * 512:(kk + 1) * 512]
                                        fn = (lambda E, o=oo, l=lhs, r=rhs,
                                              st=(j == 0), sp=(j == nj - 1):
                                              E.matmul(o, l, r, start=st,
                                                       stop=sp))
                                        if j == nj - 1 and kk == 1:
                                            te_u2 = p.op("tensor", fn, inc="TE")
                                        else:
                                            p.op("tensor", fn)
                                p.wait("vector", "TE", te_u2)
                                accsl = s_acc[:, rt * D + u * 1024:
                                              rt * D + (u + 1) * 1024]
                                if gi == 0:
                                    p.wait("vector", "ST",
                                           t3["h1_st"][4 * half + 3])
                                    t3["op_ve"][ub] = p.op(
                                        "vector", lambda E, o=accsl,
                                        i=p_op[ub][:, :]: E.tensor_copy(o, i),
                                        inc="VE")
                                else:
                                    t3["op_ve"][ub] = p.op(
                                        "vector", lambda E, o=accsl,
                                        i=p_op[ub][:, :]: E.tensor_add(o, o, i),
                                        inc="VE")
                        for ft in grp:
                            f_trk["hT_te"][ft % 5] = p.cnt["TE"]
                            f_trk["dw_te"][(2 * ft) % 7] = p.cnt["TE"]
                            f_trk["dw_te"][(2 * ft + 1) % 7] = p.cnt["TE"]

                    t3["h1bf_te"] = p.cnt["TE"]
                    for rt in range(4):
                        accsl, row_g, sc_norm = ln_block(rt, half, True,
                                                         s_h1T[:, 0:D])
                        p.wait("sync", "SC", sc_norm)
                        p.dma("sync", y[128 * row_g:128 * (row_g + 1), :], accsl,
                              inc="ST")
                    t3["acc_free_st"] = p.cnt["ST"]
                    t3["op_ve"] = [p.cnt["VE"], p.cnt["VE"]]
                gc = gate_counts()
                for e in ("tensor", "scalar", "vector", "sync"):
                    gate_wait(e, gc)

            p.wait("sync", "ST", p.cnt["ST"])

        with nc.Block() as blk:
            @blk.tensor
            def _(E):
                for f in p.eng["tensor"]:
                    f(E)

            @blk.scalar
            def _(E):
                for f in p.eng["scalar"]:
                    f(E)

            @blk.vector
            def _(E):
                for f in p.eng["vector"]:
                    f(E)

            @blk.sync
            def _(E):
                for f in p.eng["sync"]:
                    f(E)

            @blk.gpsimd
            def _(E):
                for f in p.eng["gpsimd"]:
                    f(E)

    return nc


_NC_CACHE = {}


def _host_layouts(inputs):
    import ml_dtypes
    g = {k: np.asarray(v) for k, v in inputs.items()}
    x = np.ascontiguousarray(g["L0_attn_hidden_states"].reshape(N, D))
    xt_t = np.ascontiguousarray(
        x.T.reshape(32, 128, NMC, 512).transpose(2, 1, 0, 3))

    def wtile(w):
        return np.ascontiguousarray(
            w.reshape(32, 128, NG, 512).transpose(2, 1, 0, 3))

    def btile(b):
        return np.ascontiguousarray(b.reshape(32, 128).T)

    def ftile(w):
        return np.ascontiguousarray(
            w.T.reshape(32, 128, FT, 128).transpose(2, 1, 0, 3))

    common = {
        "wq_t": wtile(g["L0_attn_Wq"]), "wk_t": wtile(g["L0_attn_Wk"]),
        "wv_t": wtile(g["L0_attn_Wv"]),
        "wo": np.ascontiguousarray(g["L0_attn_Wo"]),
        "bq_l": btile(g["L0_attn_bq"]), "bk_l": btile(g["L0_attn_bk"]),
        "bv_b": np.ascontiguousarray(np.tile(
            g["L0_attn_bv"].reshape(1, D).astype(np.float32), (128, 1))),
        "bo_r": np.ascontiguousarray(g["L0_attn_bo"].reshape(1, D)),
        "gq_c": np.ascontiguousarray(g["L0_attn_gq"].reshape(128, 1)),
        "gk_c": np.ascontiguousarray(g["L0_attn_gk"].reshape(128, 1)),
        "gw_t": ftile(g["L0_ffn_gate_w"]), "uw_t": ftile(g["L0_ffn_up_w"]),
        "dw_t": np.ascontiguousarray(g["L0_ffn_down_w"].T.reshape(FT, 128, D)),
        "gs_l": np.ascontiguousarray(g["L0_ffn_gate_s"].reshape(FT, 128).T),
        "us_l": np.ascontiguousarray(g["L0_ffn_up_s"].reshape(FT, 128).T),
        "ds_r": np.ascontiguousarray(g["L0_ffn_down_s"].reshape(1, D)),
        "ones_bf": np.ones((128, 128), ml_dtypes.bfloat16),
        "eps_c": np.tile(np.array([[RMS_EPS, LN_EPS]], np.float32), (128, 1)),
        "ones_f": np.ones((1, 128), np.float32),
        "ident": np.eye(128).astype(ml_dtypes.bfloat16),
    }
    in_maps = []
    for c in range(8):
        perm = [(2 * c + j) % NMC for j in range(NMC)]
        m = dict(common)
        m["xt"] = np.ascontiguousarray(xt_t[perm])
        m["xr"] = np.ascontiguousarray(x[R * c:R * (c + 1)])
        in_maps.append(m)
    return in_maps


def kernel(**inputs):
    in_maps = _host_layouts(inputs)
    if "nc" not in _NC_CACHE:
        _NC_CACHE["nc"] = build_nc()
    nc = _NC_CACHE["nc"]
    res = run_bass_kernel_spmd(nc, in_maps, core_ids=list(range(8)))
    out = np.concatenate([np.asarray(res.results[c]["y"]) for c in range(8)],
                         axis=0)
    return out.reshape(1, N, D).astype(np.float32)


# version 10 (final)
# speedup vs baseline: 1.0813x; 1.0813x over previous
import os
import numpy as np
import concourse.bass as bass
import concourse.mybir as mybir
from concourse.bass_utils import run_bass_kernel_spmd
from contextlib import ExitStack

F32 = mybir.dt.float32
BF16 = mybir.dt.bfloat16
I32 = mybir.dt.int32
AF = mybir.ActivationFunctionType
ALU = mybir.AluOpType
AX = mybir.AxisListType

H = 32
D = 4096
N = 8192
R = 1024
DFF = 11008
FT = 86
NMC = 16
NG = 8
SCL = 0.08838834764831845
RMS_EPS = 1e-6
LN_EPS = 1e-5


class Prog:
    def __init__(self, sems):
        self.eng = {e: [] for e in ("tensor", "scalar", "vector", "sync", "gpsimd")}
        self.sems = sems
        self.cnt = {k: 0 for k in sems}

    def wait(self, e, sem, val):
        if val and val > 0:
            s = self.sems[sem]
            self.eng[e].append(lambda E, s=s, v=val: E.wait_ge(s, v))

    def op(self, e, fn, inc=None, inc_by=1):
        if inc is None:
            self.eng[e].append(fn)
            return None
        self.cnt[inc] += inc_by
        s = self.sems[inc]
        self.eng[e].append(lambda E, f=fn, s=s, v=inc_by: f(E).then_inc(s, v))
        return self.cnt[inc]

    def dma(self, e, out_ap, in_ap, inc):
        return self.op(e, lambda E, o=out_ap, i=in_ap: E.dma_start(out=o, in_=i),
                       inc=inc, inc_by=16)


def _flat(t):
    ap = t if isinstance(t, bass.AP) else t.ap()
    if len(ap.shape) == 3:
        return ap.rearrange("p a b -> p (a b)")
    return ap


def build_nc():
    nc = bass.Bass("TRN2", target_bir_lowering=False)

    xt_f = nc.dram_tensor("xt", [NMC, 128, 32, 512], F32, kind="ExternalInput")
    xr = nc.dram_tensor("xr", [R, D], F32, kind="ExternalInput")
    wq_t = nc.dram_tensor("wq_t", [NG, 128, 32, 512], F32, kind="ExternalInput")
    wk_t = nc.dram_tensor("wk_t", [NG, 128, 32, 512], F32, kind="ExternalInput")
    wv_t = nc.dram_tensor("wv_t", [NG, 128, 32, 512], F32, kind="ExternalInput")
    wo = nc.dram_tensor("wo", [D, D], F32, kind="ExternalInput")
    bq_l = nc.dram_tensor("bq_l", [128, 32], F32, kind="ExternalInput")
    bk_l = nc.dram_tensor("bk_l", [128, 32], F32, kind="ExternalInput")
    bv_b = nc.dram_tensor("bv_b", [128, D], F32, kind="ExternalInput")
    bo_r = nc.dram_tensor("bo_r", [1, D], F32, kind="ExternalInput")
    gq_c = nc.dram_tensor("gq_c", [128, 1], F32, kind="ExternalInput")
    gk_c = nc.dram_tensor("gk_c", [128, 1], F32, kind="ExternalInput")
    gw_t = nc.dram_tensor("gw_t", [FT, 128, 32, 128], I32, kind="ExternalInput")
    uw_t = nc.dram_tensor("uw_t", [FT, 128, 32, 128], I32, kind="ExternalInput")
    dw_t = nc.dram_tensor("dw_t", [FT, 128, D], I32, kind="ExternalInput")
    gs_l = nc.dram_tensor("gs_l", [128, FT], F32, kind="ExternalInput")
    us_l = nc.dram_tensor("us_l", [128, FT], F32, kind="ExternalInput")
    ds_r = nc.dram_tensor("ds_r", [1, D], F32, kind="ExternalInput")
    ones_bf = nc.dram_tensor("ones_bf", [128, 128], BF16, kind="ExternalInput")
    ones_f = nc.dram_tensor("ones_f", [1, 128], F32, kind="ExternalInput")
    ident = nc.dram_tensor("ident", [128, 128], BF16, kind="ExternalInput")
    eps_c = nc.dram_tensor("eps_c", [128, 2], F32, kind="ExternalInput")

    y = nc.dram_tensor("y", [R, D], F32, kind="ExternalOutput")

    _dbg = "ExternalOutput" if os.environ.get("KDBG") else "Internal"
    xt_bf = nc.dram_tensor("xt_bf", [NMC, 128, 32, 512], BF16, kind=_dbg)
    kt_d = nc.dram_tensor("kt_d", [D, N], BF16, kind=_dbg)
    v2_d = nc.dram_tensor("v2_d", [H, 128, 64, 128], BF16, kind=_dbg)
    qt_d = nc.dram_tensor("qt_d", [D, R], BF16, kind=_dbg)
    at_d = nc.dram_tensor("at_d", [D, R], BF16, kind=_dbg)
    h1_d = nc.dram_tensor("h1_d", [R, D], F32, kind=_dbg)
    q2_d = nc.dram_tensor("q2_d", [H, 1024], F32, kind=_dbg)
    op_d = nc.dram_tensor("op_d", [R, D], F32, kind=_dbg)
    st_d = nc.dram_tensor("st_d", [8, 8, 128], F32, kind=_dbg)
    pre_d = nc.dram_tensor("pre_d", [R, D], F32, kind=_dbg)
    wo_bf = nc.dram_tensor("wo_bf", [D, D], BF16, kind=_dbg)
    gw_bf = nc.dram_tensor("gw_bf", [FT, 128, 32, 128], BF16)
    uw_bf = nc.dram_tensor("uw_bf", [FT, 128, 32, 128], BF16)
    dw_bf = nc.dram_tensor("dw_bf", [FT, 128, D], BF16)

    root = ExitStack()
    with root:
        sems = {k: root.enter_context(nc.semaphore(k))
                for k in ("LD", "ST", "SG", "TE", "SC", "VE")}
        p = Prog(sems)

        def sb(stk, n, s, d):
            return stk.enter_context(nc.sbuf_tensor(n, s, d))

        def ps(stk, n, s, d):
            return stk.enter_context(nc.psum_tensor(n, s, d))

        s_ones_bf = sb(root, "c_ones_bf", [128, 128], BF16)
        s_ones_f = sb(root, "c_ones_f", [1, 128], F32)
        s_ident = sb(root, "c_ident", [128, 128], BF16)
        s_bq = sb(root, "c_bq", [128, 32], F32)
        s_bk = sb(root, "c_bk", [128, 32], F32)
        s_gq = sb(root, "c_gq", [128, 1], F32)
        s_gk = sb(root, "c_gk", [128, 1], F32)
        s_gs = sb(root, "c_gs", [128, FT], F32)
        s_us = sb(root, "c_us", [128, FT], F32)
        s_k2m = sb(root, "c_k2m", [1, H], F32)
        s_eps = sb(root, "c_eps", [128, 2], F32)

        st_consts = 0
        for src, dst in ((ones_bf, s_ones_bf), (ones_f, s_ones_f), (ident, s_ident),
                         (bq_l, s_bq), (bk_l, s_bk), (gq_c, s_gq),
                         (gk_c, s_gk), (gs_l, s_gs), (us_l, s_us),
                         (eps_c, s_eps)):
            st_consts = p.dma("sync", dst[:, :], src[:, :], inc="ST")

        cv_stack = ExitStack()
        s_cv = [sb(cv_stack, f"c_cv{i}", [128, 2048], BF16) for i in range(2)]
        conv_jobs = []
        for mc in range(NMC):
            sf = _flat(xt_f[mc, :, :, :])
            df = _flat(xt_bf[mc, :, :, :])
            for q in range(8):
                conv_jobs.append((sf[:, q * 2048:(q + 1) * 2048],
                                  df[:, q * 2048:(q + 1) * 2048],
                                  ("xt", mc)))
        for hdc in range(32):
            for q in range(2):
                conv_jobs.append((wo[128 * hdc:128 * (hdc + 1),
                                     q * 2048:(q + 1) * 2048],
                                  wo_bf[128 * hdc:128 * (hdc + 1),
                                        q * 2048:(q + 1) * 2048],
                                  "wo" if hdc == 31 and q == 1 else None))
        for ft in range(FT):
            for t_, tb in ((gw_t, gw_bf), (uw_t, uw_bf), (dw_t, dw_bf)):
                sf = _flat(t_[ft]) if len(t_.shape) == 4 else t_[ft, :, :]
                df = _flat(tb[ft]) if len(tb.shape) == 4 else tb[ft, :, :]
                for q in range(2):
                    conv_jobs.append((sf[:, q * 2048:(q + 1) * 2048],
                                      df[:, q * 2048:(q + 1) * 2048], None))
        cst = {"i": 0, "st": [0, 0], "wo_st": 0}
        xt_conv_st = [0] * NMC

        def emit_conv(k):
            for _ in range(k):
                i = cst["i"]
                if i >= len(conv_jobs):
                    return
                src, dst, tag = conv_jobs[i]
                b = i % 2
                w = src.shape[-1]
                p.wait("gpsimd", "SG", cst["st"][b])
                ld = p.dma("gpsimd", s_cv[b][:, 0:w], src, inc="LD")
                p.wait("gpsimd", "LD", ld)
                st = p.dma("gpsimd", dst, s_cv[b][:, 0:w], inc="SG")
                cst["st"][b] = st
                if tag == "wo":
                    cst["wo_st"] = st
                elif tag is not None:
                    xt_conv_st[tag[1]] = st
                cst["i"] = i + 1

        emit_conv(16)

        s1 = ExitStack()
        with s1:
            s_w = {nm: sb(s1, f"a_{nm}", [128, 32 * 512], BF16)
                   for nm in ("wq", "wk", "wv")}
            s_x = [sb(s1, f"a_x{i}", [128, 16 * 512], BF16) for i in range(4)]
            s_kraw = [sb(s1, f"a_kraw{i}", [128, 512], F32) for i in range(4)]
            s_ksq = [sb(s1, f"a_ksq{i}", [128, 512], BF16) for i in range(2)]
            s_t = [sb(s1, f"a_t{i}", [128, 512], F32) for i in range(1)]
            s_rr = [sb(s1, f"a_r{i}", [128, 512], F32) for i in range(2)]
            s_kn = [sb(s1, f"a_kn{i}", [128, 512], BF16) for i in range(4)]
            s_k2sq = [sb(s1, f"a_k2sq{i}", [128, 512], BF16) for i in range(2)]
            s_vn = [sb(s1, f"a_vn{i}", [128, 512], BF16) for i in range(4)]
            s_bvb = [sb(s1, f"a_bvb{i}", [128, 512], F32) for i in range(2)]
            s_red = sb(s1, "a_red", [1, 1], F32)
            s_q2r = [sb(s1, f"a_q2r{i}", [1, 512], F32) for i in range(2)]

            p_kv = [ps(s1, f"a_pkv{i}", [128, 512], F32) for i in range(4)]
            p_ms = [ps(s1, f"a_pms{i}", [128, 512], F32) for i in range(2)]
            p_k2 = ps(s1, "a_pk2", [1, 512], F32)

            trk = {
                "w_free": {"wk": 0, "wv": 0, "wq": 0},
                "x_free_te": [0, 0, 0, 0], "wld": {},
                "bvb_ve": [0, 0], "bvb_st": [0, 0],
                "kraw_ve": [0, 0, 0, 0], "ksq_te": [0, 0], "ms_sc": [0, 0],
                "r_ve": [0, 0], "kn_st": [0, 0, 0, 0], "k2sq_te": [0, 0],
                "k2_ve": 0, "vn_st": [0, 0, 0, 0],
                "kv_free": [("SC", 0)] * 4, "kv_done": [0, 0, 0, 0],
                "par": 0, "q2r_st": [0, 0], "x_ld_st": {},
            }

            def rhs_x(it, dc):
                buf = (2 * it + dc // 16) % 4
                o = (dc % 16) * 512
                return s_x[buf][:, o:o + 512]

            def wave(wname, it):
                for h in range(4):
                    sem, cv = trk["kv_free"][h]
                    p.wait("tensor", sem, cv)
                    for dc in range(32):
                        lhs = s_w[wname][:, dc * 512 + h * 128:dc * 512 + (h + 1) * 128]
                        rhs = rhs_x(it, dc)
                        if dc < 31:
                            p.op("tensor", lambda E, o=p_kv[h][:, :], l=lhs, r=rhs,
                                 st=(dc == 0): E.matmul(o, l, r, start=st, stop=False))
                        else:
                            trk["kv_done"][h] = p.op(
                                "tensor", lambda E, o=p_kv[h][:, :], l=lhs, r=rhs:
                                E.matmul(o, l, r, start=False, stop=True), inc="TE")

            def evac_kraw(g, h, kind):
                head = 4 * g + h
                bias = (s_bk if kind == "k" else s_bq)[:, head:head + 1]
                p.wait("scalar", "TE", trk["kv_done"][h])
                p.wait("scalar", "VE", trk["kraw_ve"][h])
                sc_kraw = p.op("scalar", lambda E, o=s_kraw[h][:, :],
                               i=p_kv[h][:, :], bb=bias:
                               E.activation(o, i, AF.Identity, bias=bb), inc="SC")
                trk["kv_free"][h] = ("SC", sc_kraw)
                return sc_kraw

            def norm_rest(g, mc, h, kind, sc_kraw):
                head = 4 * g + h
                b2 = trk["par"] % 2
                trk["par"] += 1
                gvec = s_gk if kind == "k" else s_gq
                p.wait("vector", "SC", sc_kraw)
                p.wait("vector", "TE", trk["ksq_te"][b2])
                ve_ksq = p.op("vector", lambda E, o=s_ksq[b2][:, :],
                              a=s_kraw[h][:, :]: E.tensor_mul(o, a, a), inc="VE")
                p.wait("tensor", "VE", ve_ksq)
                p.wait("tensor", "SC", trk["ms_sc"][b2])
                te_ms = p.op("tensor", lambda E, o=p_ms[b2][:, :],
                             l=s_ones_bf[:, :], r=s_ksq[b2][:, :]:
                             E.matmul(o, l, r, start=True, stop=True), inc="TE")
                trk["ksq_te"][b2] = te_ms
                p.wait("scalar", "TE", te_ms)
                p.wait("scalar", "VE", trk["r_ve"][b2])
                p.op("scalar", lambda E, o=s_t[0][:, :], i=p_ms[b2][:, :]:
                     E.activation(o, i, AF.Ln, bias=s_eps[:, 0:1],
                                  scale=1.0 / 128.0))
                sc_r = p.op("scalar", lambda E, o=s_rr[b2][:, :], i=s_t[0][:, :]:
                            E.activation(o, i, AF.Exp, scale=-0.5), inc="SC")
                trk["ms_sc"][b2] = sc_r
                p.wait("vector", "SC", sc_r)
                p.wait("vector", "ST", trk["kn_st"][h])
                ve_kn = p.op("vector", lambda E, o=s_kn[h][:, :],
                             a=s_kraw[h][:, :], gg=gvec[:, :], rr2=s_rr[b2][:, :]:
                             E.scalar_tensor_tensor(o, a, gg, rr2, ALU.mult,
                                                    ALU.mult), inc="VE")
                trk["kraw_ve"][h] = ve_kn
                trk["r_ve"][b2] = ve_kn
                p.wait("vector", "TE", trk["k2sq_te"][b2])
                ve_k2 = p.op("vector", lambda E, o=s_k2sq[b2][:, :],
                             a=s_kn[h][:, :]: E.tensor_mul(o, a, a), inc="VE")
                p.wait("tensor", "VE", ve_k2)
                p.wait("tensor", "VE", trk["k2_ve"])
                te_k2 = p.op("tensor", lambda E, o=p_k2[:, :],
                             l=s_ones_bf[:, 0:1], r=s_k2sq[b2][:, :]:
                             E.matmul(o, l, r, start=True, stop=True), inc="TE")
                trk["k2sq_te"][b2] = te_k2
                p.wait("vector", "TE", te_k2)
                if kind == "k":
                    ve_red = p.op("vector", lambda E, o=s_red[:, :], i=p_k2[:, :]:
                                  E.reduce_max(o, i, axis=AX.X), inc="VE")
                    p.wait("vector", "VE", ve_red)
                    if mc == 0:
                        ve_m = p.op("vector", lambda E,
                                    o=s_k2m[0:1, head:head + 1], i=s_red[:, :]:
                                    E.tensor_copy(o, i), inc="VE")
                    else:
                        ve_m = p.op("vector", lambda E,
                                    o=s_k2m[0:1, head:head + 1], i=s_red[:, :]:
                                    E.tensor_max(o, o, i), inc="VE")
                else:
                    qb = trk["par"] % 2
                    p.wait("vector", "ST", trk["q2r_st"][qb])
                    ve_m = p.op("vector", lambda E, o=s_q2r[qb][:, :],
                                i=p_k2[:, :]: E.tensor_copy(o, i), inc="VE")
                    p.wait("sync", "VE", ve_m)
                    trk["q2r_st"][qb] = p.dma(
                        "sync", q2_d[head:head + 1, mc * 512:mc * 512 + 512],
                        s_q2r[qb][:, :], inc="ST")
                trk["k2_ve"] = ve_m
                dst = kt_d if kind == "k" else qt_d
                col = mc * 512
                p.wait("sync", "VE", ve_kn)
                trk["kn_st"][h] = p.dma(
                    "sync", dst[128 * head:128 * (head + 1), col:col + 512],
                    s_kn[h][:, :], inc="ST")

            def load_x_halves(it):
                if it >= NG * NMC:
                    return
                mc = it % NMC
                assert xt_conv_st[mc] > 0, "conv emission fell behind"
                flat = _flat(xt_bf[mc, :, :, :])
                for half in range(2):
                    buf = (2 * it + half) % 4
                    p.wait("sync", "SG", xt_conv_st[mc])
                    p.wait("sync", "TE", trk["x_free_te"][buf])
                    trk["x_ld_st"][it] = p.dma(
                        "sync", s_x[buf][:, :],
                        flat[:, half * 8192:(half + 1) * 8192], inc="ST")

            load_x_halves(0)
            for g in range(NG):
                for nm, t in (("wq", wq_t), ("wk", wk_t), ("wv", wv_t)):
                    p.wait("gpsimd", "TE", trk["w_free"][nm])
                    trk["wld"][nm] = p.dma("gpsimd", s_w[nm][:, :],
                                           _flat(t[g, :, :, :]), inc="LD")
                gb = g % 2
                p.wait("sync", "VE", trk["bvb_ve"][gb])
                trk["bvb_st"][gb] = p.dma(
                    "sync", s_bvb[gb][:, :], bv_b[:, 512 * g:512 * (g + 1)],
                    inc="ST")
                for mc in range(NMC):
                    it = g * NMC + mc
                    load_x_halves(it + 1)
                    p.wait("tensor", "ST", trk["x_ld_st"][it])
                    if mc == 0:
                        p.wait("tensor", "LD", trk["wld"]["wk"])
                    wave("wk", it)
                    if mc == NMC - 1:
                        trk["w_free"]["wk"] = p.cnt["TE"]
                    sc_k = [evac_kraw(g, h, "k") for h in range(4)]
                    if mc == 0:
                        p.wait("tensor", "LD", trk["wld"]["wv"])
                        p.wait("tensor", "ST", trk["bvb_st"][g % 2])
                    for mt in range(4):
                        sem, cv = trk["kv_free"][mt]
                        p.wait("tensor", sem, cv)
                        for dc in range(32):
                            lhs_x = rhs_x(it, dc)[:, mt * 128:(mt + 1) * 128]
                            rhs_w = s_w["wv"][:, dc * 512:(dc + 1) * 512]
                            if dc < 31:
                                p.op("tensor", lambda E, o=p_kv[mt][:, :],
                                     l=lhs_x, r=rhs_w, st=(dc == 0):
                                     E.matmul(o, l, r, start=st, stop=False))
                            else:
                                trk["kv_done"][mt] = p.op(
                                    "tensor", lambda E, o=p_kv[mt][:, :],
                                    l=lhs_x, r=rhs_w:
                                    E.matmul(o, l, r, start=False, stop=True),
                                    inc="TE")
                    for mt in range(4):
                        mtg = mc * 4 + mt
                        p.wait("vector", "TE", trk["kv_done"][mt])
                        p.wait("vector", "ST", trk["vn_st"][mt])
                        ve_v = p.op("vector", lambda E, o=s_vn[mt][:, :],
                                    i=p_kv[mt][:, :], bb=s_bvb[g % 2][:, :]:
                                    E.tensor_add(o, i, bb), inc="VE")
                        trk["kv_free"][mt] = ("VE", ve_v)
                        p.wait("sync", "VE", ve_v)
                        st_v = 0
                        for h in range(4):
                            head = 4 * g + h
                            st_v = p.dma(
                                "sync",
                                v2_d[head, :, mtg, :].rearrange("p hd -> p 1 hd")
                                if False else v2_d[head, :, mtg:mtg + 1, :],
                                s_vn[mt].ap().rearrange("p (h hd) -> p h hd",
                                                        hd=128)[:, h:h + 1, :],
                                inc="ST")
                        trk["vn_st"][mt] = st_v
                    trk["bvb_ve"][g % 2] = max(trk["bvb_ve"][g % 2],
                                               p.cnt["VE"])
                    if mc == NMC - 1:
                        trk["w_free"]["wv"] = p.cnt["TE"]
                    for h in range(4):
                        norm_rest(g, mc, h, "k", sc_k[h])
                    if mc < 2:
                        if mc == 0:
                            p.wait("tensor", "LD", trk["wld"]["wq"])
                        wave("wq", it)
                        if mc == 1:
                            trk["w_free"]["wq"] = p.cnt["TE"]
                        sc_q = [evac_kraw(g, h, "q") for h in range(4)]
                        for h in range(4):
                            norm_rest(g, mc, h, "q", sc_q[h])
                    trk["x_free_te"][(2 * it) % 4] = p.cnt["TE"]
                    trk["x_free_te"][(2 * it + 1) % 4] = p.cnt["TE"]
                    emit_conv(8 if it < 16 else 2)

            s1_te, s1_sc, s1_ve, s1_st = (p.cnt["TE"], p.cnt["SC"],
                                          p.cnt["VE"], p.cnt["ST"])


        s2 = ExitStack()
        with s2:
            s_kt = [sb(s2, f"b_kt{i}", [128, N], BF16) for i in range(2)]
            s_vna = [sb(s2, f"b_vnat{i}", [128, N], BF16) for i in range(2)]
            s_qt = [sb(s2, f"b_qt{i}", [128, R], BF16) for i in range(2)]
            s_E = [sb(s2, f"b_E{i}", [128, 1024], BF16) for i in range(2)]
            s_rS = sb(s2, "b_rS", [128, 512], F32)
            s_oT = sb(s2, "b_outT", [128, R], BF16)
            s_Bt = [sb(s2, f"b_Bt{i}", [1, R], F32) for i in range(2)]
            s_nb = sb(s2, "b_negb", [1, 2], F32)
            s_nB = sb(s2, "b_negB", [128, 2], F32)

            p_sc = [ps(s2, f"b_psc{i}", [128, 1024], F32) for i in range(2)]
            p_pv = [ps(s2, f"b_ppv{i}", [128, 512], F32) for i in range(2)]
            p_S = [ps(s2, f"b_pS{i}", [128, 512], F32) for i in range(2)]

            t2 = {"kvq_te": [s1_te, s1_te],
                  "E_te": [0, 0], "sc_sc": [s1_sc, s1_sc], "oT_st": 0,
                  "pvS_ve": s1_ve}

            p.wait("tensor", "SC", s1_sc)
            p.wait("tensor", "VE", s1_ve)

            def head_loads(h):
                hb = h % 2
                p.wait("sync", "ST", s1_st)
                p.wait("sync", "TE", t2["kvq_te"][hb])
                p.dma("sync", s_kt[hb][:, :], kt_d[128 * h:128 * (h + 1), :],
                      inc="ST")
                p.dma("sync", s_vna[hb][:, :], _flat(v2_d[h, :, :, :]), inc="ST")
                p.dma("sync", s_qt[hb][:, :], qt_d[128 * h:128 * (h + 1), :],
                      inc="ST")
                p.wait("sync", "VE", t2["bt_ve"][hb])
                t2["ld"][hb] = p.dma("sync", s_Bt[hb][:, :], q2_d[h:h + 1, :],
                                     inc="ST")

            t2["ld"] = [0, 0]
            t2["bt_ve"] = [0, 0]
            head_loads(0)
            for h in range(H):
                hb = h % 2
                if h + 1 < H:
                    head_loads(h + 1)
                ld_k = ld_v = ld_q = t2["ld"][hb]

                if h == 0:
                    p.wait("vector", "SC", s1_sc)
                p.wait("vector", "ST", ld_q)
                ve_bt = p.op("vector", lambda E, o=s_Bt[hb][:, :],
                             i=s_Bt[hb][:, :], km=s_k2m[0:1, h:h + 1]:
                             E.tensor_scalar(o, i, km, 1.0 / 128.0, ALU.mult,
                                             ALU.mult), inc="VE")
                p.wait("scalar", "VE", ve_bt)
                p.op("scalar", lambda E, o=s_Bt[hb][:, :], i=s_Bt[hb][:, :]:
                     E.activation(o, i, AF.Ln))
                sc_bt = p.op("scalar", lambda E, o=s_Bt[hb][:, :],
                             i=s_Bt[hb][:, :]:
                             E.activation(o, i, AF.Exp, scale=0.5), inc="SC")
                p.wait("vector", "SC", sc_bt)
                ve_nb0 = p.op("vector", lambda E, o=s_nb[0:1, 0:1],
                              i=s_Bt[hb][0:1, 0:512]:
                              E.reduce_max(o, i, axis=AX.X, negate=True), inc="VE")
                p.wait("vector", "VE", ve_nb0)
                ve_nb = p.op("vector", lambda E, o=s_nb[0:1, 1:2],
                             i=s_Bt[hb][0:1, 512:1024]:
                             E.reduce_max(o, i, axis=AX.X, negate=True), inc="VE")
                t2["bt_ve"][hb] = ve_nb
                p.wait("tensor", "VE", ve_nb)
                p.wait("tensor", "VE", t2["pvS_ve"])
                te_nb = p.op("tensor", lambda E, o=p_pv[0][:, 0:2],
                             l=s_ones_f[:, :], r=s_nb[:, :]:
                             E.matmul(o, l, r, start=True, stop=True), inc="TE")
                p.wait("vector", "TE", te_nb)
                ve_nB = p.op("vector", lambda E, o=s_nB[:, :], i=p_pv[0][:, 0:2]:
                             E.tensor_copy(o, i), inc="VE")

                p.wait("tensor", "ST", ld_q)
                ve_vnat = ve_nB
                te_attn = [0, 0]
                for bq in range(32):
                    for ncx in range(2):
                        eb = ncx
                        p.wait("tensor", "SC", t2["sc_sc"][eb])
                        te_sc_last = 0
                        for k in range(2):
                            mt = 2 * bq + k
                            lhs = s_kt[hb][:, mt * 128:(mt + 1) * 128]
                            rhs = s_qt[hb][:, ncx * 512:(ncx + 1) * 512]
                            oo = p_sc[eb][:, k * 512:(k + 1) * 512]
                            fn = (lambda E, o=oo, l=lhs, r=rhs:
                                  E.matmul(o, l, r, start=True, stop=True))
                            te_sc_last = p.op("tensor", fn,
                                              inc="TE" if k == 1 else None) \
                                or te_sc_last
                        p.wait("scalar", "TE", te_sc_last)
                        p.wait("scalar", "TE", t2["E_te"][eb])
                        if bq == 0 and ncx == 0:
                            p.wait("scalar", "VE", ve_nB)
                        sc_e = p.op("scalar", lambda E, o=s_E[eb][:, :],
                                    i=p_sc[eb][:, :], bb=s_nB[:, ncx:ncx + 1]:
                                    E.activation(o, i, AF.Exp, bias=bb, scale=SCL),
                                    inc="SC")
                        t2["sc_sc"][eb] = sc_e
                        p.wait("tensor", "SC", sc_e)
                        if bq == 0:
                            p.wait("tensor", "VE", ve_vnat)
                            p.wait("tensor", "VE", t2["pvS_ve"])
                        first = (bq == 0)
                        last = (bq == 31)
                        for k in range(2):
                            mt = 2 * bq + k
                            p.op("tensor", lambda E, o=p_pv[ncx][:, :],
                                 l=s_vna[hb][:, mt * 128:(mt + 1) * 128],
                                 r=s_E[eb][:, k * 512:(k + 1) * 512],
                                 st=(first and k == 0), sp=(last and k == 1):
                                 E.matmul(o, l, r, start=st, stop=sp))
                        for k in range(2):
                            fn = (lambda E, o=p_S[ncx][:, :], l=s_ones_bf[:, :],
                                  r=s_E[eb][:, k * 512:(k + 1) * 512],
                                  st=(first and k == 0), sp=(last and k == 1):
                                  E.matmul(o, l, r, start=st, stop=sp))
                            if last and k == 1:
                                te_attn[ncx] = p.op("tensor", fn, inc="TE")
                            else:
                                p.op("tensor", fn)
                        t2["E_te"][eb] = p.cnt["TE"]
                for ncx in range(2):
                    p.wait("vector", "TE", te_attn[ncx])
                    p.op("vector", lambda E, o=s_rS[:, :], i=p_S[ncx][:, :]:
                         E.reciprocal(o, i))
                    p.wait("vector", "ST", t2["oT_st"])
                    t2["pvS_ve"] = p.op(
                        "vector", lambda E, o=s_oT[:, ncx * 512:(ncx + 1) * 512],
                        a=p_pv[ncx][:, :], b=s_rS[:, :]: E.tensor_mul(o, a, b),
                        inc="VE")
                t2["kvq_te"][hb] = p.cnt["TE"]
                p.wait("sync", "VE", t2["pvS_ve"])
                t2["oT_st"] = p.dma("sync", at_d[128 * h:128 * (h + 1), :],
                                    s_oT[:, :], inc="ST")
                emit_conv(12)

            s2_te, s2_sc, s2_ve, s2_st = (p.cnt["TE"], p.cnt["SC"],
                                          p.cnt["VE"], p.cnt["ST"])
        while cst["i"] < len(conv_jobs):
            emit_conv(16)
        conv_st = p.cnt["SG"]
        wo_conv_st = cst["wo_st"]
        cv_stack.close()

        s3 = ExitStack()
        with s3:
            s_acc = sb(s3, "d_acc", [128, 4 * D], F32)
            s_h1T = sb(s3, "d_h1T", [128, 32 * 512], BF16)
            s_aux = sb(s3, "d_aux", [128, D], F32)
            s_bo_b = sb(s3, "d_bo_b", [128, D], BF16)
            s_ds_b = sb(s3, "d_ds_b", [128, D], BF16)
            lnb = {nm: sb(s3, f"d_ln_{nm}", [128, 1], F32)
                   for nm in ("sum", "ssq", "m", "t3", "var", "rstd", "lb")}

            p_op = [ps(s3, f"d_pop{i}", [128, 1024], F32) for i in range(2)]
            p_g = ps(s3, "d_pg", [128, 512], F32)
            p_u = ps(s3, "d_pu", [128, 512], F32)
            p_tp3 = [ps(s3, f"d_ptp3{i}", [128, 128], BF16) for i in range(2)]

            t3 = {"op_ve": [s2_ve, s2_ve], "aux_ve": s2_ve,
                  "tp3_ve": [s2_ve, s2_ve], "h1bf_te": s2_te, "h1bf_sc": s2_sc,
                  "h1_st": [0] * 8, "acc_free_st": 0, "bcast_ve": s2_ve}

            p.wait("tensor", "SC", s2_sc)
            p.wait("tensor", "VE", s2_ve)

            bc = ExitStack()
            with bc:
                s_row = sb(bc, "d_row", [1, D], F32)
                for which, (rowsrc, dstb) in enumerate(((bo_r, s_bo_b),
                                                        (ds_r, s_ds_b))):
                    p.wait("sync", "VE", t3["bcast_ve"] if which else 0)
                    row_ld = p.dma("sync", s_row[:, :], rowsrc[:, :], inc="ST")
                    for j4 in range(4):
                        cc = j4 * 1024
                        p.wait("tensor", "ST", row_ld)
                        p.wait("tensor", "VE", t3["bcast_ve"])
                        p.op("tensor", lambda E, o=p_op[1][:, 0:512],
                             l=s_ones_f[:, :], r=s_row[0:1, cc:cc + 512]:
                             E.matmul(o, l, r, start=True, stop=True))
                        te_b = p.op("tensor", lambda E, o=p_op[1][:, 512:1024],
                                    l=s_ones_f[:, :],
                                    r=s_row[0:1, cc + 512:cc + 1024]:
                                    E.matmul(o, l, r, start=True, stop=True),
                                    inc="TE")
                        p.wait("vector", "TE", te_b)
                        t3["bcast_ve"] = p.op(
                            "vector", lambda E, o=dstb[:, cc:cc + 1024],
                            i=p_op[1][:, :]: E.tensor_copy(o, i), inc="VE")
            ve_bcast = t3["bcast_ve"]
            t3["op_ve"] = [ve_bcast, ve_bcast]

            GDN = 3
            groups = [list(range(s, min(s + GDN, FT))) for s in range(0, FT, GDN)]

            def gate_counts():
                return dict(TE=p.cnt["TE"], SC=p.cnt["SC"], VE=p.cnt["VE"],
                            ST=p.cnt["ST"], LD=p.cnt["LD"])

            def gate_wait(e, gc):
                for s, v in gc.items():
                    p.wait(e, s, v)

            for half in range(2):
                rbase = half * 512
                gc = gate_counts()
                op_stack = ExitStack()
                with op_stack:
                    s_at = [sb(op_stack, f"d_at{half}{i}", [128, 4 * 512], BF16)
                            for i in range(2)]
                    s_wo4 = [sb(op_stack, f"d_wo{half}{i}", [128, 4 * 2048], BF16)
                             for i in range(2)]
                    s_h1bf = sb(op_stack, f"d_h1bf{half}", [128, D], BF16)
                    for e in ("tensor", "scalar", "vector", "sync"):
                        gate_wait(e, gc)
                    op_trk = {"at_te": [0, 0]}
                    p.wait("sync", "ST", s2_st)
                    p.wait("sync", "SG", wo_conv_st)
                    for dmh in range(2):
                        for gg in range(8):
                            wb = gg % 2
                            p.wait("sync", "TE", op_trk["at_te"][wb])
                            for j in range(4):
                                hdc = gg * 4 + j
                                p.dma("sync", s_at[wb][:, j * 512:(j + 1) * 512],
                                      at_d[128 * hdc:128 * (hdc + 1),
                                           rbase:rbase + 512], inc="ST")
                            wo_ld = 0
                            for j in range(4):
                                hdc = gg * 4 + j
                                wo_ld = p.dma(
                                    "sync", s_wo4[wb][:, j * 2048:(j + 1) * 2048],
                                    wo_bf[128 * hdc:128 * (hdc + 1),
                                          dmh * 2048:(dmh + 1) * 2048], inc="ST")
                            p.wait("tensor", "ST", wo_ld)
                            for rt in range(4):
                                for u in range(2):
                                    ub = u % 2
                                    p.wait("tensor", "VE", t3["op_ve"][ub])
                                    te_u = 0
                                    for j in range(4):
                                        for kk in range(2):
                                            lhs = s_at[wb][:, j * 512 + rt * 128:
                                                           j * 512 + (rt + 1) * 128]
                                            rhs = s_wo4[wb][
                                                :, j * 2048 + u * 1024 + kk * 512:
                                                j * 2048 + u * 1024 + (kk + 1) * 512]
                                            oo = p_op[ub][:, kk * 512:(kk + 1) * 512]
                                            fn = (lambda E, o=oo, l=lhs, r=rhs,
                                                  st=(j == 0), sp=(j == 3):
                                                  E.matmul(o, l, r, start=st,
                                                           stop=sp))
                                            if j == 3 and kk == 1:
                                                te_u = p.op("tensor", fn, inc="TE")
                                            else:
                                                p.op("tensor", fn)
                                    p.wait("vector", "TE", te_u)
                                    accsl = s_acc[:, rt * D + dmh * 2048 + u * 1024:
                                                  rt * D + dmh * 2048 +
                                                  (u + 1) * 1024]
                                    if gg == 0:
                                        p.wait("vector", "ST", t3["acc_free_st"])
                                        t3["op_ve"][ub] = p.op(
                                            "vector", lambda E, o=accsl,
                                            i=p_op[ub][:, :]: E.tensor_copy(o, i),
                                            inc="VE")
                                    else:
                                        t3["op_ve"][ub] = p.op(
                                            "vector", lambda E, o=accsl,
                                            i=p_op[ub][:, :]: E.tensor_add(o, o, i),
                                            inc="VE")
                            op_trk["at_te"][wb] = p.cnt["TE"]

                    def ln_block(rt, half, is_ln2, dump):
                        accsl = s_acc[:, rt * D:(rt + 1) * D]
                        row_g = half * 4 + rt
                        p.wait("sync", "VE", t3["aux_ve"])
                        if is_ln2:
                            p.wait("sync", "ST", t3["h1_st"][4 * half + rt])
                            src_d = h1_d
                        else:
                            src_d = xr
                        ld_a = p.dma("sync", s_aux[:, :],
                                     src_d[128 * row_g:128 * (row_g + 1), :],
                                     inc="ST")
                        if rt == 0:
                            p.wait("vector", "VE", t3["op_ve"][0])
                            p.wait("vector", "VE", t3["op_ve"][1])
                        if is_ln2:
                            p.op("vector", lambda E, o=accsl, b=s_ds_b[:, :]:
                                 E.tensor_mul(o, o, b))
                        else:
                            p.op("vector", lambda E, o=accsl, b=s_bo_b[:, :]:
                                 E.tensor_add(o, o, b))
                        p.wait("vector", "ST", ld_a)
                        t3["aux_ve"] = p.op("vector", lambda E, o=accsl,
                                            b=s_aux[:, :]: E.tensor_add(o, o, b),
                                            inc="VE")
                        p.wait("scalar", "VE", t3["aux_ve"])
                        p.wait("scalar", "TE", t3["h1bf_te"])
                        sc_sum = p.op("scalar", lambda E, o=dump, i=accsl,
                                      ao=lnb["sum"][:, :]:
                                      E.activation(o, i, AF.Identity, accum_out=ao),
                                      inc="SC")
                        t3["h1bf_sc"] = sc_sum
                        sc_ssq = p.op("scalar", lambda E, o=dump, i=accsl,
                                      ao=lnb["ssq"][:, :]:
                                      E.activation(o, i, AF.Square, accum_out=ao),
                                      inc="SC")
                        p.wait("vector", "SC", sc_ssq)
                        ve_m = p.op("vector", lambda E, o=lnb["m"][:, :],
                                    i=lnb["sum"][:, :]:
                                    E.tensor_scalar_mul(o, i, 1.0 / D), inc="VE")
                        p.wait("vector", "VE", ve_m)
                        ve_t3 = p.op("vector", lambda E, o=lnb["t3"][:, :],
                                     i=lnb["m"][:, :]: E.tensor_mul(o, i, i),
                                     inc="VE")
                        p.wait("vector", "VE", ve_t3)
                        ve_var = p.op("vector", lambda E, o=lnb["var"][:, :],
                                      i=lnb["ssq"][:, :], t=lnb["t3"][:, :]:
                                      E.scalar_tensor_tensor(o, i, 1.0 / D, t,
                                                             ALU.mult,
                                                             ALU.subtract),
                                      inc="VE")
                        p.wait("scalar", "VE", ve_var)
                        sc_ln = p.op("scalar", lambda E, o=lnb["rstd"][:, :],
                                     i=lnb["var"][:, :]:
                                     E.activation(o, i, AF.Ln, bias=s_eps[:, 1:2]),
                                     inc="SC")
                        p.wait("scalar", "SC", sc_ln)
                        sc_rstd = p.op("scalar", lambda E, o=lnb["rstd"][:, :],
                                       i=lnb["rstd"][:, :]:
                                       E.activation(o, i, AF.Exp, scale=-0.5),
                                       inc="SC")
                        p.wait("vector", "SC", sc_rstd)
                        ve_lb = p.op("vector", lambda E, o=lnb["lb"][:, :],
                                     i=lnb["m"][:, :], rs=lnb["rstd"][:, :]:
                                     E.scalar_tensor_tensor(o, i, -1.0, rs,
                                                            ALU.mult, ALU.mult),
                                     inc="VE")
                        p.wait("scalar", "VE", ve_lb)
                        p.wait("scalar", "SC", sc_rstd)
                        if os.environ.get("KDBG") and not is_ln2:
                            p.wait("sync", "VE", ve_lb)
                            for ki, nm in enumerate(("sum", "ssq", "m", "t3",
                                                     "var", "rstd", "lb")):
                                p.dma("sync", st_d[row_g, ki, :].rearrange(
                                    "(p o) -> p o", o=1), lnb[nm][:, :], inc="ST")
                            dbg_st = p.dma("sync",
                                           pre_d[128 * row_g:128 * (row_g + 1), :],
                                           accsl, inc="ST")
                            p.wait("scalar", "ST", dbg_st)
                        sc_norm = p.op("scalar", lambda E, o=accsl, i=accsl,
                                       bb=lnb["lb"][:, :], ss=lnb["rstd"][:, :]:
                                       E.activation(o, i, AF.Identity, bias=bb,
                                                    scale=ss), inc="SC")
                        return accsl, row_g, sc_norm

                    if os.environ.get("KDBG"):
                        for rt in range(4):
                            row_g = half * 4 + rt
                            p.wait("sync", "VE", t3["op_ve"][0])
                            p.wait("sync", "VE", t3["op_ve"][1])
                            p.dma("sync", op_d[128 * row_g:128 * (row_g + 1), :],
                                  s_acc[:, rt * D:(rt + 1) * D], inc="ST")
                    for rt in range(4):
                        accsl, row_g, sc_norm = ln_block(rt, half, False,
                                                          s_h1bf[:, :])
                        p.wait("sync", "SC", sc_norm)
                        t3["h1_st"][4 * half + rt] = p.dma(
                            "sync", h1_d[128 * row_g:128 * (row_g + 1), :], accsl,
                            inc="ST")
                        p.wait("vector", "SC", sc_norm)
                        ve_h1bf = p.op("vector", lambda E, o=s_h1bf[:, :],
                                       i=accsl: E.tensor_copy(o, i), inc="VE")
                        p.wait("tensor", "VE", ve_h1bf)
                        for dc in range(32):
                            tb3 = dc % 2
                            p.wait("tensor", "VE", t3["tp3_ve"][tb3])
                            te_tp = p.op("tensor", lambda E, o=p_tp3[tb3][:, :],
                                         i=s_h1bf[:, dc * 128:(dc + 1) * 128],
                                         idd=s_ident[:, :]:
                                         E.transpose(o, i, idd), inc="TE")
                            p.wait("vector", "TE", te_tp)
                            t3["tp3_ve"][tb3] = p.op(
                                "vector", lambda E,
                                o=s_h1T[:, dc * 512 + rt * 128:
                                        dc * 512 + (rt + 1) * 128],
                                i=p_tp3[tb3][:, :]: E.tensor_copy(o, i), inc="VE")
                        t3["h1bf_te"] = p.cnt["TE"]
                    ve_h1T_done = max(t3["tp3_ve"])

                gc = gate_counts()
                ffn_stack = ExitStack()
                with ffn_stack:
                    s_gw = [sb(ffn_stack, f"d_gw{half}{i}", [128, 32 * 128], BF16)
                            for i in range(2)]
                    s_uw = [sb(ffn_stack, f"d_uw{half}{i}", [128, 32 * 128], BF16)
                            for i in range(2)]
                    s_dwh = [sb(ffn_stack, f"d_dwh{half}{i}", [128, 2048], BF16)
                             for i in range(7)]
                    s_gsil = sb(ffn_stack, f"d_gsil{half}", [128, 512], F32)
                    s_sup = sb(ffn_stack, f"d_sup{half}", [128, 512], F32)
                    s_hT = [sb(ffn_stack, f"d_hT{half}{i}", [128, 512], BF16)
                            for i in range(5)]
                    for e in ("tensor", "scalar", "vector", "sync"):
                        gate_wait(e, gc)
                    f_trk = {"gw_te": [0, 0], "dw_te": [0] * 7, "hT_te": [0] * 5,
                             "gu_sc": [0, 0], "gs_ve": 0}
                    for gi, grp in enumerate(groups):
                        for ft in grp:
                            fb = ft % 2
                            if gi == 0 and ft == 0:
                                p.wait("sync", "SG", conv_st)
                            p.wait("sync", "TE", f_trk["gw_te"][fb])
                            p.dma("sync", s_gw[fb][:, :],
                                  _flat(gw_bf[ft, :, :, :]), inc="ST")
                            uw_ld = p.dma("sync", s_uw[fb][:, :],
                                          _flat(uw_bf[ft, :, :, :]), inc="ST")
                            for dh in range(2):
                                di = (2 * ft + dh) % 7
                                p.wait("sync", "TE", f_trk["dw_te"][di])
                                p.dma("sync", s_dwh[di][:, :],
                                      dw_bf[ft, :, dh * 2048:(dh + 1) * 2048],
                                      inc="ST")
                            p.wait("tensor", "ST", uw_ld)
                            if gi == 0 and ft == grp[0]:
                                p.wait("tensor", "VE", ve_h1T_done)
                            p.wait("tensor", "SC", f_trk["gu_sc"][0])
                            te_g = 0
                            for dc in range(32):
                                lhs = s_gw[fb][:, dc * 128:(dc + 1) * 128]
                                rhs = s_h1T[:, dc * 512:dc * 512 + 512]
                                fn = (lambda E, o=p_g[:, :], l=lhs, r=rhs,
                                      st=(dc == 0), sp=(dc == 31):
                                      E.matmul(o, l, r, start=st, stop=sp))
                                te_g = p.op("tensor", fn,
                                            inc="TE" if dc == 31 else None) or te_g
                            p.wait("tensor", "SC", f_trk["gu_sc"][1])
                            te_up = 0
                            for dc in range(32):
                                lhs = s_uw[fb][:, dc * 128:(dc + 1) * 128]
                                rhs = s_h1T[:, dc * 512:dc * 512 + 512]
                                fn = (lambda E, o=p_u[:, :], l=lhs, r=rhs,
                                      st=(dc == 0), sp=(dc == 31):
                                      E.matmul(o, l, r, start=st, stop=sp))
                                te_up = p.op("tensor", fn,
                                             inc="TE" if dc == 31 else None) \
                                    or te_up
                            f_trk["gw_te"][fb] = p.cnt["TE"]
                            p.wait("scalar", "TE", te_g)
                            p.wait("scalar", "VE", f_trk["gs_ve"])
                            sc_g = p.op("scalar", lambda E, o=s_gsil[:, :],
                                        i=p_g[:, :], ss=s_gs[:, ft:ft + 1]:
                                        E.activation(o, i, AF.Silu, scale=ss),
                                        inc="SC")
                            f_trk["gu_sc"][0] = sc_g
                            p.wait("scalar", "TE", te_up)
                            sc_u = p.op("scalar", lambda E, o=s_sup[:, :],
                                        i=p_u[:, :], ss=s_us[:, ft:ft + 1]:
                                        E.activation(o, i, AF.Identity, scale=ss),
                                        inc="SC")
                            f_trk["gu_sc"][1] = sc_u
                            p.wait("vector", "SC", sc_u)
                            p.wait("vector", "TE", f_trk["hT_te"][ft % 5])
                            f_trk["gs_ve"] = p.op(
                                "vector", lambda E, o=s_hT[ft % 5][:, :],
                                a=s_gsil[:, :], b=s_sup[:, :]:
                                E.tensor_mul(o, a, b), inc="VE")
                        ve_h_grp = f_trk["gs_ve"]
                        p.wait("tensor", "VE", ve_h_grp)
                        nj = len(grp)
                        for rt in range(4):
                            for u in range(4):
                                ub = u % 2
                                p.wait("tensor", "VE", t3["op_ve"][ub])
                                te_u2 = 0
                                for j, ft in enumerate(grp):
                                    di = (2 * ft + (u // 2)) % 7
                                    for kk in range(2):
                                        lhs = s_hT[ft % 5][:, rt * 128:
                                                           (rt + 1) * 128]
                                        rhs = s_dwh[di][:, (u % 2) * 1024 +
                                                        kk * 512:
                                                        (u % 2) * 1024 +
                                                        (kk + 1) * 512]
                                        oo = p_op[ub][:, kk * 512:(kk + 1) * 512]
                                        fn = (lambda E, o=oo, l=lhs, r=rhs,
                                              st=(j == 0), sp=(j == nj - 1):
                                              E.matmul(o, l, r, start=st,
                                                       stop=sp))
                                        if j == nj - 1 and kk == 1:
                                            te_u2 = p.op("tensor", fn, inc="TE")
                                        else:
                                            p.op("tensor", fn)
                                p.wait("vector", "TE", te_u2)
                                accsl = s_acc[:, rt * D + u * 1024:
                                              rt * D + (u + 1) * 1024]
                                if gi == 0:
                                    p.wait("vector", "ST",
                                           t3["h1_st"][4 * half + 3])
                                    t3["op_ve"][ub] = p.op(
                                        "vector", lambda E, o=accsl,
                                        i=p_op[ub][:, :]: E.tensor_copy(o, i),
                                        inc="VE")
                                else:
                                    t3["op_ve"][ub] = p.op(
                                        "vector", lambda E, o=accsl,
                                        i=p_op[ub][:, :]: E.tensor_add(o, o, i),
                                        inc="VE")
                        for ft in grp:
                            f_trk["hT_te"][ft % 5] = p.cnt["TE"]
                            f_trk["dw_te"][(2 * ft) % 7] = p.cnt["TE"]
                            f_trk["dw_te"][(2 * ft + 1) % 7] = p.cnt["TE"]

                    t3["h1bf_te"] = p.cnt["TE"]
                    for rt in range(4):
                        accsl, row_g, sc_norm = ln_block(rt, half, True,
                                                         s_h1T[:, 0:D])
                        p.wait("sync", "SC", sc_norm)
                        p.dma("sync", y[128 * row_g:128 * (row_g + 1), :], accsl,
                              inc="ST")
                    t3["acc_free_st"] = p.cnt["ST"]
                    t3["op_ve"] = [p.cnt["VE"], p.cnt["VE"]]
                gc = gate_counts()
                for e in ("tensor", "scalar", "vector", "sync"):
                    gate_wait(e, gc)

            p.wait("sync", "ST", p.cnt["ST"])

        with nc.Block() as blk:
            @blk.tensor
            def _(E):
                for f in p.eng["tensor"]:
                    f(E)

            @blk.scalar
            def _(E):
                for f in p.eng["scalar"]:
                    f(E)

            @blk.vector
            def _(E):
                for f in p.eng["vector"]:
                    f(E)

            @blk.sync
            def _(E):
                for f in p.eng["sync"]:
                    f(E)

            @blk.gpsimd
            def _(E):
                for f in p.eng["gpsimd"]:
                    f(E)

    return nc


_NC_CACHE = {}


def _host_layouts(inputs):
    import ml_dtypes
    g = {k: np.asarray(v) for k, v in inputs.items()}
    x = np.ascontiguousarray(g["L0_attn_hidden_states"].reshape(N, D))
    xt_t = np.ascontiguousarray(
        x.T.reshape(32, 128, NMC, 512).transpose(2, 1, 0, 3))

    def wtile(w):
        return np.ascontiguousarray(
            w.reshape(32, 128, NG, 512).transpose(2, 1, 0, 3))

    def btile(b):
        return np.ascontiguousarray(b.reshape(32, 128).T)

    def ftile(w):
        return np.ascontiguousarray(
            w.T.reshape(32, 128, FT, 128).transpose(2, 1, 0, 3))

    common = {
        "wq_t": wtile(g["L0_attn_Wq"]), "wk_t": wtile(g["L0_attn_Wk"]),
        "wv_t": wtile(g["L0_attn_Wv"]),
        "wo": np.ascontiguousarray(g["L0_attn_Wo"]),
        "bq_l": btile(g["L0_attn_bq"]), "bk_l": btile(g["L0_attn_bk"]),
        "bv_b": np.ascontiguousarray(np.tile(
            g["L0_attn_bv"].reshape(1, D).astype(np.float32), (128, 1))),
        "bo_r": np.ascontiguousarray(g["L0_attn_bo"].reshape(1, D)),
        "gq_c": np.ascontiguousarray(g["L0_attn_gq"].reshape(128, 1)),
        "gk_c": np.ascontiguousarray(g["L0_attn_gk"].reshape(128, 1)),
        "gw_t": ftile(g["L0_ffn_gate_w"]), "uw_t": ftile(g["L0_ffn_up_w"]),
        "dw_t": np.ascontiguousarray(g["L0_ffn_down_w"].T.reshape(FT, 128, D)),
        "gs_l": np.ascontiguousarray(g["L0_ffn_gate_s"].reshape(FT, 128).T),
        "us_l": np.ascontiguousarray(g["L0_ffn_up_s"].reshape(FT, 128).T),
        "ds_r": np.ascontiguousarray(g["L0_ffn_down_s"].reshape(1, D)),
        "ones_bf": np.ones((128, 128), ml_dtypes.bfloat16),
        "eps_c": np.tile(np.array([[RMS_EPS, LN_EPS]], np.float32), (128, 1)),
        "ones_f": np.ones((1, 128), np.float32),
        "ident": np.eye(128).astype(ml_dtypes.bfloat16),
    }
    in_maps = []
    for c in range(8):
        perm = [(2 * c + j) % NMC for j in range(NMC)]
        m = dict(common)
        m["xt"] = np.ascontiguousarray(xt_t[perm])
        m["xr"] = np.ascontiguousarray(x[R * c:R * (c + 1)])
        in_maps.append(m)
    return in_maps


def kernel(**inputs):
    in_maps = _host_layouts(inputs)
    if "nc" not in _NC_CACHE:
        _NC_CACHE["nc"] = build_nc()
    nc = _NC_CACHE["nc"]
    res = run_bass_kernel_spmd(nc, in_maps, core_ids=list(range(8)))
    out = np.concatenate([np.asarray(res.results[c]["y"]) for c in range(8)],
                         axis=0)
    return out.reshape(1, N, D).astype(np.float32)
